# version 22
# baseline (speedup 1.0000x reference)
import contextlib

import numpy as np

import concourse.bass as bass
import concourse.mybir as mybir
from concourse.bass_utils import run_bass_kernel_spmd

F32 = mybir.dt.float32
BF16 = mybir.dt.bfloat16
I32 = mybir.dt.int32
AF = mybir.ActivationFunctionType
ALU = mybir.AluOpType
AX = mybir.AxisListType.X

S = 4096
D = 1024
HL = 8
DL = 256
NKC = S // 128
NQC = S // 512
SCALE = float(1.0 / np.sqrt(np.float64(96.0)))
CLIP_LO = float(np.exp(np.float64(-80.0) / np.sqrt(np.float64(96.0))))
CLIP_HI = float(np.exp(np.float64(80.0) / np.sqrt(np.float64(96.0))))
LN_EPS = 1e-5
CLAMP_MAX = float(np.log(np.float64(100.0)))

ENGS = ("tensor", "vector", "scalar", "sync", "gpsimd")
T, V, SG, SY, G = ENGS

DEBUG_TAPS = []


class B:

    def __init__(self, nc, es):
        self.nc = nc
        self.q = {e: [] for e in ENGS}
        self.sem = {e: es.enter_context(nc.semaphore("ch_" + e)) for e in ENGS}
        self.cnt = {e: 0 for e in ENGS}
        self.used = {e: set() for e in ENGS}
        self.dsem = {e: es.enter_context(nc.semaphore("dma_" + e))
                     for e in ("sync", "gpsimd")}
        self.dcnt = {e: 0 for e in ("sync", "gpsimd")}
        self.es = es
        self.ssem = {}
        self.scnt = {}

    def on(self, eng, fn):
        self.q[eng].append((fn, 0, None))

    def op(self, eng, fn, step=1):
        self.cnt[eng] += step
        self.q[eng].append((fn, step, "op"))
        return self.cnt[eng]

    def wt(self, eng, src, tok):
        assert tok <= self.cnt[src], (eng, src, tok, self.cnt[src])
        if tok <= 0:
            return
        self.used[src].add(tok)
        sem = self.sem[src]
        self.q[eng].append((lambda e, s=sem, v=tok: e.wait_ge(s, v), 0, None))

    def dma(self, eng, out, in_):
        return self.dma_group(eng, [(out, in_)])

    def dma_group(self, eng, pairs):
        dsem = self.dsem[eng]
        for (out, in_) in pairs:
            self.dcnt[eng] += 16
            self.q[eng].append((
                lambda e, o=out, i=in_, ds=dsem:
                e.dma_start(out=o, in_=i).then_inc(ds, 16), 0, None))
        dtarget = self.dcnt[eng]
        self.q[eng].append((lambda e, ds=dsem, t=dtarget: e.wait_ge(ds, t),
                            0, None))
        self.cnt[eng] += 1
        self.q[eng].append((None, 1, "pub"))
        return self.cnt[eng]

    def dma_slot(self, eng, key, pairs):
        if (eng, key) not in self.ssem:
            self.ssem[(eng, key)] = self.es.enter_context(
                self.nc.semaphore("sd_%s_%s" % (eng, key)))
            self.scnt[(eng, key)] = 0
        sem = self.ssem[(eng, key)]
        for (out, in_) in pairs:
            self.scnt[(eng, key)] += 16
            self.q[eng].append((
                lambda e, o=out, i=in_, ds=sem:
                e.dma_start(out=o, in_=i).then_inc(ds, 16), 0, None))
        return ("d", (eng, key), self.scnt[(eng, key)])

    def wtd(self, eng, tok):
        _, ek, val = tok
        sem = self.ssem[ek]
        self.q[eng].append((lambda e, s=sem, v=val: e.wait_ge(s, v), 0, None))

    def sw(self, eng):
        self.wt(eng, eng, self.cnt[eng])

    def join(self):
        snap = dict(self.cnt)
        ssnap = dict(self.scnt)
        for e in ENGS:
            for o in ENGS:
                if o != e and snap[o] > 0:
                    self.wt(e, o, snap[o])
            for ek, v in ssnap.items():
                if v > 0:
                    self.wtd(e, ("d", ek, v))

    def finalize(self, block):
        for eng in ENGS:
            queue, sem = self.q[eng], self.sem[eng]
            used = self.used[eng]
            plan = []
            val = 0
            pending = 0
            for (fn, step, kind) in queue:
                val += step
                pending += step
                if step > 0 and val in used:
                    plan.append((fn, pending, kind))
                    pending = 0
                elif kind == "pub":
                    plan.append((None, 0, None))
                else:
                    plan.append((fn, 0, kind))

            def emit(e, plan=plan, sem=sem):
                for fn, inc, kind in plan:
                    if kind == "pub":
                        e.sem_inc(sem, inc)
                        continue
                    if fn is None:
                        continue
                    ins = fn(e)
                    if inc:
                        ins.then_inc(sem, inc)

            getattr(block, eng)(emit)


class Slot:

    def __init__(self, b):
        self.b = b
        self.guards = []

    def acquire(self, writer_eng):
        for (e, t) in self.guards:
            if isinstance(t, tuple):
                self.b.wtd(writer_eng, t)
            elif e != writer_eng:
                self.b.wt(writer_eng, e, t)
        self.guards = []

    def release(self, eng, tok):
        for i, (e, t) in enumerate(self.guards):
            if e == eng and not isinstance(t, tuple) and not isinstance(tok, tuple):
                self.guards[i] = (e, max(t, tok))
                return
        self.guards.append((eng, tok))


def build_program():
    nc = bass.Bass("TRN2", target_bir_lowering=False)

    def dt(n, shp, ty=F32, kind="ExternalInput"):
        return nc.dram_tensor(n, shp, ty, kind=kind)

    x_d = dt("x", [S, D]).ap()
    xres_d = dt("xres", [S // 2, D]).ap()
    wdq_d = dt("wdq", [D, DL]).ap()
    wdkv_d = dt("wdkv", [D, DL]).ap()
    wkr_d = dt("wkr", [D, 32]).ap()
    wuq_d = dt("wuq", [DL, 512]).ap()
    wqr_d = dt("wqr", [DL, 256]).ap()
    wuk_d = dt("wuk", [DL, 512]).ap()
    wuv_d = dt("wuv", [DL, 512]).ap()
    wo_d = dt("wo", [D, D]).ap()
    wff_d = dt("wff", [D, D]).ap()
    wp_d = dt("wp", [DL, D]).ap()
    bp_d = dt("bp", [DL]).ap()
    sim_d = dt("sim", [DL, 4]).ap()
    temp_d = dt("temp", [1]).ap()
    wd_d = dt("wd", [15, D]).ap()
    wu_d = dt("wu", [D, 15]).ap()
    cosq_d = dt("cosq", [128, S]).ap()
    sinq_d = dt("sinq", [128, S]).ap()
    cosk_d = dt("cosk", [128, 1024]).ap()
    sink_d = dt("sink", [128, 1024]).ap()
    ident_d = dt("ident", [128, 128]).ap()
    p128_d = dt("p128", [128, 128]).ap()
    triu_d = dt("triu", [128, 128]).ap()
    e415_d = dt("e415", [4, 15]).ap()
    hoff_d = dt("hoff", [1, 16], I32).ap()
    o_d = dt("o", [S // 2, D], F32, kind="ExternalOutput").ap()
    cc_inA_h = dt("cc_inA", [4, 128, S // 2], BF16, kind="Internal")
    cc_inB_h = dt("cc_inB", [4, 128, S // 2], BF16, kind="Internal")
    cc_outA_h = dt("cc_outA", [2, 4, 128, S // 2], BF16, kind="Internal")
    cc_outB_h = dt("cc_outB", [2, 4, 128, S // 2], BF16, kind="Internal")
    cc_inA = cc_inA_h.ap()
    cc_inB = cc_inB_h.ap()
    ccrb_h = dt("ccrb", [128, 16 * 2048], BF16, kind="Internal")
    dbg_d = {}
    for tap in DEBUG_TAPS:
        name, _, (p0, p1), (c0, c1), ty, phase = tap
        dbg_d[name] = dt("dbg_" + name, [p1 - p0, c1 - c0], ty,
                         kind="ExternalOutput").ap()

    es = contextlib.ExitStack()
    b = B(nc, es)
    sbuf = {}

    def emit_taps(phase):
        if not DEBUG_TAPS:
            return
        todo = [t for t in DEBUG_TAPS if t[5] == phase]
        if not todo:
            return
        b.join()
        for (name, sbn, (p0, p1), (c0, c1), ty, _) in todo:
            b.dma(G, dbg_d[name], sbuf[sbn][p0:p1, c0:c1])
        b.join()

    def mk_sb(stack):
        def f(n, shp, ty=BF16):
            t = stack.enter_context(nc.sbuf_tensor(n, shp, ty))
            sbuf[n] = t
            return t
        return f

    def psum(n, shp):
        return es.enter_context(nc.psum_tensor(n, shp, F32))

    PS = [psum("PS0", [128, 1024]), psum("PS1", [128, 1024])]
    PM = [psum("PM0", [128, 512]), psum("PM1", [128, 512])]
    PA = [psum("PA0", [128, 512]), psum("PA1", [128, 512])]
    PSB = [PS[0][:].bitcast(BF16), PS[1][:].bitcast(BF16)]
    PMB = [PM[0][:].bitcast(BF16), PM[1][:].bitcast(BF16)]
    PA1B = PA[1][:].bitcast(BF16)
    sPS = [Slot(b), Slot(b)]
    sPM = [Slot(b), Slot(b)]
    sPA = [Slot(b), Slot(b)]

    sbr = mk_sb(es)
    identF = sbr("identF", [128, 128], F32)
    identB = sbr("identB", [128, 128])
    p128b = sbr("p128b", [128, 128])
    triub = sbr("triub", [128, 128])
    e415b = sbr("e415b", [4, 15])
    onesb = sbr("onesb", [128, 1])
    ones64f = sbr("ones64f", [65, 64], F32)
    ones64b = sbr("ones64b", [65, 64])
    ones1f = sbr("ones1f", [1, 128], F32)
    hoffs = sbr("hoffs", [1, 16], I32)
    scaleT = sbr("scaleT", [1, 1], F32)
    smalls = sbr("smalls", [128, 64], F32)

    def sc(i):
        return smalls[:, i:i + 1]

    es_a = contextlib.ExitStack()
    sba = mk_sb(es_a)
    c_kvT = sba("c_kvT", [128, 2 * S])
    c_qT = sba("c_qT", [128, 2 * S])
    krTb = sba("krTb", [128, 1024])
    wUQb = sba("wUQb", [128, 2 * 512])
    wQRb = sba("wQRb", [128, 2 * 256])
    wUKb = sba("wUKb", [128, 2 * 512])
    wUVb = sba("wUVb", [128, 2 * 512])
    QT = sba("QT", [128, 4 * HL * 512])
    pbuf = sba("pbuf", [128, 4 * 1024])
    stage = sba("stage", [64, 3 * 512])
    oddst = sba("oddst", [128, 2 * 512])
    qrraw = sba("qrraw", [128, 2 * 512])
    qrrope = sba("qrrope", [128, 2 * 512])
    tmpq = sba("tmpq", [128, 512], F32)
    den = sba("den", [65, 2 * 512], F32)
    denb = sba("denb", [65, 2 * 1024])
    repsb = sba("repsb", [64, 2 * 512], F32)
    cosqs = sba("cosqs", [128, 512], F32)
    sinqs = sba("sinqs", [128, 512], F32)

    sQTl = [Slot(b) for _ in range(4)]
    spb = [Slot(b), Slot(b), Slot(b), Slot(b)]
    sstage = [Slot(b) for _ in range(3)]
    soddst = [Slot(b), Slot(b)]
    sden = [Slot(b), Slot(b)]
    srepsb = [Slot(b), Slot(b)]
    stabs = [Slot(b)]
    sqr = [Slot(b), Slot(b)]

    es_a0 = contextlib.ExitStack()
    sba0 = mk_sb(es_a0)
    wstage = sba0("wstage", [128, 4 * 2048], F32)
    wDQb = sba0("wDQb", [128, 8 * DL])
    wDKVb = sba0("wDKVb", [128, 8 * DL])
    wKRb = sba0("wKRb", [128, 8 * 32])
    xst = sba0("xst", [128, 4 * 1024], F32)
    htok = sba0("htok", [128, 4 * 1024])
    junkA = sba0("junkA", [128, 1024], F32)
    hT = sba0("hT", [128, 2 * 4096])
    krTraw = sba0("krTraw", [128, 1024])
    krstage = sba0("krstage", [32, 2 * 512])
    coskA = sba0("coskA", [128, 1024], F32)
    sinkA = sba0("sinkA", [128, 1024], F32)
    swst = [Slot(b) for _ in range(4)]
    sxst = [Slot(b) for _ in range(4)]
    shtok = [Slot(b) for _ in range(4)]
    shT = [Slot(b), Slot(b)]
    skrst = [Slot(b), Slot(b)]
    sstat = [Slot(b) for _ in range(4)]

    b.dma(SY, identF[:], ident_d)
    b.dma(SY, wstage[:, 0:128], p128_d)
    b.dma(SY, wstage[:, 128:256], triu_d)
    tcd = b.dma(SY, wstage[0:4, 256:271], e415_d)
    b.wt(V, SY, tcd)
    b.op(V, lambda e: e.tensor_copy(identB[:], identF[:]))
    b.op(V, lambda e: e.tensor_copy(p128b[:], wstage[:, 0:128]))
    b.op(V, lambda e: e.tensor_copy(triub[:], wstage[:, 128:256]))
    b.op(V, lambda e: e.tensor_copy(e415b[:], wstage[0:4, 256:271]))
    b.op(V, lambda e: e.memset(onesb[:], 1.0))
    b.op(V, lambda e: e.memset(ones64f[:], 0.0))
    b.sw(V)
    b.op(V, lambda e: e.memset(ones64f[64:65, :], 1.0))
    b.op(V, lambda e: e.memset(ones64b[:], 0.0))
    b.sw(V)
    b.op(V, lambda e: e.memset(ones64b[64:65, :], 1.0))
    tcv = b.op(V, lambda e: e.memset(ones1f[:], 1.0))
    for _sw in swst:
        _sw.release(V, tcv)

    wtok = {}
    wpend = {}

    def weight_issue(widx):
        name, src, dst, rows, cols = wq[widx]
        kk = rows // 128
        sl = widx % 4
        swst[sl].acquire(G)
        for k0 in range(0, kk, 2):
            k1 = min(kk, k0 + 2)
            td = b.dma_slot(
                G, "wst%d" % sl,
                [(wstage[:, sl * 2048 + k0 * cols:
                         sl * 2048 + k1 * cols].rearrange(
                             "p (k c) -> p k c", k=k1 - k0),
                  src[k0 * 128:k1 * 128, :].rearrange("(k p) c -> p k c",
                                                      p=128))])
        wpend[name] = (td, sl)

    def weight_cast(widx):
        name, src, dst, rows, cols = wq[widx]
        tot = (rows // 128) * cols
        td, sl = wpend[name]
        b.wtd(V, td)
        for c0 in range(0, tot, 1024):
            c1 = min(tot, c0 + 1024)
            tc_ = b.op(V, lambda e, sl=sl, d=dst, c0=c0, c1=c1:
                       e.tensor_copy(d[:, c0:c1],
                                     wstage[:, sl * 2048 + c0:
                                            sl * 2048 + c1]))
        swst[sl].release(V, tc_)
        wtok[name] = tc_

    tt0 = b.dma(SY, scaleT[:], temp_d.rearrange("(a c) -> a c", a=1))
    b.wt(V, SY, tt0)
    tt1 = b.op(V, lambda e: e.tensor_single_scalar(scaleT[:], scaleT[:],
                                                   CLAMP_MAX, ALU.min))
    b.wt(SG, V, tt1)
    b.op(SG, lambda e: e.activation(scaleT[:], scaleT[:], AF.Exp))

    tok_kr_dma = []
    st1 = {}
    st3 = {}
    ev_v = {}
    ev_s = {}

    def a0_stage1(i):
        xsl = i % 4
        ss = xsl * 16
        sxst[xsl].acquire(SY)
        tx = b.dma_slot(SY, "x%d" % xsl,
                        [(xst[:, xsl * 1024:(xsl + 1) * 1024],
                          x_d[i * 128:(i + 1) * 128, :])])
        xt = xst[:, xsl * 1024:(xsl + 1) * 1024]
        b.wtd(V, tx)
        sstat[xsl].acquire(V)
        b.op(V, lambda e, xt=xt, o=sc(ss + 0): e.reduce_sum(o, xt, axis=AX))
        tms = b.op(V, lambda e, o=sc(ss + 1): e.memset(o, 0.0))
        b.wtd(SG, tx)
        b.wt(SG, V, tms)
        tsq = b.op(SG, lambda e, xt=xt, o=sc(ss + 1):
                   e.activation(junkA[:], xt, AF.Square, accum_out=o))
        st1[i] = tsq

    def a0_stage2(i):
        xsl = i % 4
        ss = xsl * 16
        b.wt(V, SG, st1[i])
        b.op(V, lambda e, a=sc(ss + 0), o=sc(ss + 2):
             e.tensor_single_scalar(o, a, -1.0 / 1024.0, ALU.mult))
        b.op(V, lambda e, a=sc(ss + 1), o=sc(ss + 3):
             e.tensor_single_scalar(o, a, 1.0 / 1024.0, ALU.mult))
        b.sw(V)
        b.op(V, lambda e, a=sc(ss + 2), o=sc(ss + 4):
             e.tensor_tensor(o, a, a, ALU.mult))
        b.sw(V)
        b.op(V, lambda e, a=sc(ss + 3), c=sc(ss + 4), o=sc(ss + 5):
             e.tensor_tensor(o, a, c, ALU.subtract))
        b.sw(V)
        tv = b.op(V, lambda e, a=sc(ss + 5), o=sc(ss + 5):
                  e.tensor_single_scalar(o, a, LN_EPS, ALU.add))
        b.wt(SG, V, tv)
        b.op(SG, lambda e, a=sc(ss + 5), o=sc(ss + 6):
             e.activation(o, a, AF.Ln))
        b.sw(SG)
        b.op(SG, lambda e, a=sc(ss + 6), o=sc(ss + 7):
             e.activation(o, a, AF.Exp, scale=-0.5))
        b.sw(SG)
        st1[i] = b.op(SG, lambda e, m=sc(ss + 2), r=sc(ss + 7),
                      o=sc(ss + 8):
                      e.activation(o, m, AF.Identity, scale=r))

    def a0_stage3(i):
        xsl = i % 4
        ss = xsl * 16
        xt = xst[:, xsl * 1024:(xsl + 1) * 1024]
        ht = htok[:, xsl * 1024:(xsl + 1) * 1024]
        b.sw(SG)
        shtok[xsl].acquire(SG)
        tap_ = b.op(SG, lambda e, xt=xt, ht=ht, r=sc(ss + 7), bi=sc(ss + 8):
                    e.activation(ht, xt, AF.Identity, bias=bi, scale=r))
        sxst[xsl].release(SG, tap_)
        sstat[xsl].release(SG, tap_)
        psl = i % 2
        b.wt(T, SG, tap_)
        sPS[psl].acquire(T)
        for k in range(8):
            ttr = b.op(T, lambda e, ht=ht, k=k, p=PSB[psl]:
                       e.transpose(p[:, k * 128:(k + 1) * 128],
                                   ht[:, k * 128:(k + 1) * 128], identB[:]))
        shtok[xsl].release(T, ttr)
        st3[i] = ttr

    def a0_stage4(i):
        i4 = i % 4
        tc = i // 4
        hsl = tc % 2
        psl = i % 2
        b.wt(V, T, st3[i])
        if i4 == 0:
            shT[hsl].acquire(V)
        dst = hT[:, hsl * 4096: (hsl + 1) * 4096].rearrange(
            "p (k t) -> p k t", k=8)[:, :, i4 * 128:(i4 + 1) * 128]
        src = PSB[psl][:, 0:1024].rearrange("p (k t) -> p k t", k=8)
        tev = b.op(V, lambda e, d=dst, s=src: e.tensor_copy(d, s))
        ev_v[tc] = tev
        sPS[psl].release(V, tev)

    proj_tok = {}

    def a0_proj_mm(tc, part):
        hsl = tc % 2
        hTc = hT[:, hsl * 4096: hsl * 4096 + 4096]
        b.wt(T, V, ev_v[tc])
        if part < 2:
            (nm, wsrc) = (("wdkv", wDKVb), ("wdq", wDQb))[part]
            tg = (sPM if part == 0 else sPA)
            pg = (PM if part == 0 else PA)
            b.wt(T, V, wtok[nm])
            for mc in range(2):
                tg[mc].acquire(T)
                for k in range(8):
                    tmm = b.op(T, lambda e, w=wsrc, mc=mc, k=k, h=hTc,
                               p=pg[mc]:
                               e.matmul(p[:],
                                        w[:, k * 256 + mc * 128:
                                          k * 256 + (mc + 1) * 128],
                                        h[:, k * 512:(k + 1) * 512],
                                        start=(k == 0), stop=(k == 7)))
                shT[hsl].release(T, tmm)
            proj_tok[(tc, part)] = tmm
        else:
            b.wt(T, V, wtok["wkr"])
            sPM[0].acquire(T)
            for k in range(8):
                tkr = b.op(T, lambda e, k=k, h=hTc:
                           e.matmul(PM[0][0:32, :],
                                    wKRb[:, k * 32:(k + 1) * 32],
                                    h[:, k * 512:(k + 1) * 512],
                                    start=(k == 0), stop=(k == 7)))
            shT[hsl].release(T, tkr)
            proj_tok[(tc, part)] = tkr

    def a0_proj_ev(tc, part):
        b.wt(V, T, proj_tok[(tc, part)])
        if part < 2:
            dstt = (c_kvT, c_qT)[part]
            tg = (sPM if part == 0 else sPA)
            pg = (PM if part == 0 else PA)
            for mc in range(2):
                tevc = b.op(V, lambda e, d=dstt, mc=mc, tc=tc, p=pg[mc]:
                            e.tensor_copy(
                                d[:, mc * S + tc * 512:
                                  mc * S + (tc + 1) * 512], p[:]))
                tg[mc].release(V, tevc)
        else:
            ksl = tc % 2
            skrst[ksl].acquire(V)
            tkv = b.op(V, lambda e, s=ksl:
                       e.tensor_copy(krstage[:, s * 512:(s + 1) * 512],
                                     PM[0][0:32, :]))
            sPM[0].release(V, tkv)
            b.wt(G, V, tkv)
            tkd = b.dma_slot(G, "krs%d" % ksl,
                             [(krTraw[32 * (tc // 2):32 * (tc // 2) + 32,
                                      (tc % 2) * 512:(tc % 2 + 1) * 512],
                               krstage[:, ksl * 512:(ksl + 1) * 512])])
            skrst[ksl].release(G, tkd)
            tok_kr_dma.append(tkd)

    wq = [("wdkv", wdkv_d, wDKVb, D, DL), ("wdq", wdq_d, wDQb, D, DL),
          ("wkr", wkr_d, wKRb, D, 32), ("wuk", wuk_d, wUKb, DL, 512),
          ("wuv", wuv_d, wUVb, DL, 512), ("wuq", wuq_d, wUQb, DL, 512),
          ("wqr", wqr_d, wQRb, DL, 256)]
    wissue = {0: 0, 1: 1, 2: 2, 3: 3, 8: 4, 9: 5, 10: 6}
    wcast = {5: 0, 6: 1, 7: 2}
    tok_cs = [None]
    sched = {}
    for i in range(39):
        for (kind, tcx, part) in sched.pop(i, []):
            if kind == "ev":
                a0_proj_ev(tcx, part)
            else:
                a0_proj_mm(tcx, part)
        if i < 32:
            a0_stage1(i)
        if 1 <= i < 33:
            a0_stage2(i - 1)
        if 2 <= i < 34:
            a0_stage3(i - 2)
        if 3 <= i < 35:
            a0_stage4(i - 3)
            if (i - 3) % 4 == 3:
                tcx = (i - 3) // 4
                sched.setdefault(i + 1, []).append(("mm", tcx, 0))
                sched.setdefault(i + 2, []).extend(
                    [("ev", tcx, 0), ("mm", tcx, 1)])
                sched.setdefault(i + 3, []).extend(
                    [("ev", tcx, 1), ("mm", tcx, 2)])
                sched.setdefault(i + 4, []).append(("ev", tcx, 2))
        if i in wcast:
            weight_cast(wcast[i])
        if i in wissue:
            weight_issue(wissue[i])
        if i == 10:
            tok_cs[0] = b.dma_slot(SY, "cs", [(coskA[:], cosk_d),
                                              (sinkA[:], sink_d),
                                              (hoffs[:], hoff_d)])
    for widx in (3, 4, 5, 6):
        weight_cast(widx)

    b.wtd(T, tok_kr_dma[-2])
    b.wtd(T, tok_kr_dma[-1])
    sPS[0].acquire(T)
    for hseg in range(2):
        tsw = b.op(T, lambda e, h=hseg:
                   e.matmul(PS[0][:, h * 512:(h + 1) * 512], p128b[:],
                            krTraw[:, h * 512:(h + 1) * 512],
                            start=True, stop=True))
    b.wt(V, T, tsw)
    b.wtd(V, tok_cs[0])
    b.op(V, lambda e: e.tensor_tensor(xst[:, 0:1024], krTraw[:], coskA[:], ALU.mult))
    b.op(V, lambda e: e.tensor_tensor(junkA[:], PS[0][:], sinkA[:], ALU.mult))
    b.sw(V)
    tkrope = b.op(V, lambda e: e.tensor_tensor(krTb[:], xst[:, 0:1024], junkA[:],
                                               ALU.add))
    sPS[0].release(V, tkrope)

    emit_taps("a0")
    b.join()
    es_a0.close()
    K_all = sba("K_all", [128, HL * S])
    V_grp = sba("V_grp", [128, NKC * HL * 65])

    def vsl(kc, h):
        base = (kc * HL + h) * 65
        return V_grp[:, base: base + 65]

    ones_ap = V_grp[:].rearrange("p (g j) -> p g j", j=65)[:, :, 64:65]
    tvg = b.op(V, lambda e: e.memset(ones_ap, 1.0))
    b.wt(T, V, wtok["wuk"])
    b.wt(T, V, wtok["wuv"])
    b.wt(T, V, tvg)
    rotp = [PM[0], PM[1], PA[0], PA[1]]
    rots = [sPM[0], sPM[1], sPA[0], sPA[1]]
    ridx = [0]

    def nextp():
        i = ridx[0] % 4
        ridx[0] += 1
        return rotp[i], rots[i]

    for run in range(8):
        for mc in range(4):
            p, psl_ = nextp()
            psl_.acquire(T)
            for k in range(2):
                tke = b.op(T, lambda e, mc=mc, k=k, run=run, p=p:
                           e.matmul(p[:],
                                    wUKb[:, k * 512 + mc * 128:
                                         k * 512 + (mc + 1) * 128],
                                    c_kvT[:, k * S + run * 512:
                                          k * S + (run + 1) * 512],
                                    start=(k == 0), stop=(k == 1)))
            b.wt(V, T, tke)
            he, ho = 2 * mc, 2 * mc + 1
            tev = b.op(V, lambda e, h=he, run=run, p=p:
                       e.tensor_copy(K_all[0:64, h * S + run * 512:
                                           h * S + (run + 1) * 512],
                                     p[0:64, :]))
            osl = (run * 4 + mc) % 2
            qeng = G if osl else SY
            b.wt(SG, T, tke)
            soddst[osl].acquire(SG)
            tex = b.op(SG, lambda e, p=p, o=osl:
                       e.copy(oddst[64:128, o * 512:(o + 1) * 512],
                              p[64:128, :]))
            psl_.release(V, tev)
            psl_.release(SG, tex)
            b.wt(qeng, SG, tex)
            tkd = b.dma_slot(qeng, "odd%d" % osl,
                             [(K_all[0:64, ho * S + run * 512:
                                     ho * S + (run + 1) * 512],
                               oddst[64:128, osl * 512:(osl + 1) * 512])])
            soddst[osl].release(qeng, tkd)
        for j in range(4):
            kc = run * 4 + j
            p, psl_ = nextp()
            psl_.acquire(T)
            for k in range(2):
                tvm = b.op(T, lambda e, kc=kc, k=k, p=p:
                           e.matmul(p[:], c_kvT[:, k * S + kc * 128:
                                                k * S + (kc + 1) * 128],
                                    wUVb[:, k * 512:(k + 1) * 512],
                                    start=(k == 0), stop=(k == 1)))
            ev_eng = V if j % 2 == 0 else SG
            b.wt(ev_eng, T, tvm)
            dstv = V_grp[:, kc * 520:(kc + 1) * 520].rearrange(
                "p (h j) -> p h j", j=65)[:, :, 0:64]
            srcv = p[:].rearrange("p (h j) -> p h j", j=64)
            if ev_eng is V:
                tvev = b.op(V, lambda e, d=dstv, s=srcv: e.tensor_copy(d, s))
            else:
                tvev = b.op(SG, lambda e, d=dstv, s=srcv: e.copy(d, s))
            psl_.release(ev_eng, tvev)
    b.wt(G, V, tkrope)
    b.dma_group(G, [(K_all[64:96, h * S + ch * 1024: h * S + (ch + 1) * 1024],
                     krTb[32 * ch:32 * ch + 32, :])
                    for h in range(HL) for ch in range(4)])
    b.join()
    emit_taps("expand")

    tok_cc_last = 0
    exp_hist = []
    av_hist = []
    pend_av = []
    pend_eps = []
    ep_ssl = [0]
    tcollA = [0]

    def av_hist_tok(k):
        return av_hist[k]

    sdiv = [Slot(b), Slot(b)]

    def emit_qexp(j2):
        qof = j2 * 512
        qsl = 0
        qb = (j2 % 4) * HL * 512
        stabs[qsl].acquire(SY)
        ttab = b.dma_slot(SY, "tb0", [
            (cosqs[:], cosq_d[:, qof:qof + 512]),
            (sinqs[:], sinq_d[:, qof:qof + 512])])
        sQTl[j2 % 4].acquire(V)
        tq_v = 0
        tg_toks = []
        for mc in range(4):
            psl = mc % 2
            sPM[psl].acquire(T)
            for k in range(2):
                tqm = b.op(T, lambda e, mc=mc, k=k, qof=qof, p=PM[psl]:
                           e.matmul(p[:],
                                    wUQb[:, k * 512 + mc * 128:
                                         k * 512 + (mc + 1) * 128],
                                    c_qT[:, k * S + qof: k * S + qof + 512],
                                    start=(k == 0), stop=(k == 1)))
            b.wt(V, T, tqm)
            he, ho = 2 * mc, 2 * mc + 1
            tq_v = b.op(V, lambda e, h=he, p=PM[psl], qb=qb:
                        e.tensor_copy(QT[0:64, qb + h * 512:
                                          qb + (h + 1) * 512],
                                      p[0:64, :]))
            osl = mc % 2
            soddst[osl].acquire(V)
            tqo = b.op(V, lambda e, p=PM[psl], o=osl:
                       e.tensor_copy(oddst[64:128, o * 512:(o + 1) * 512],
                                     p[64:128, :]))
            sPM[psl].release(V, tqo)
            b.wt(G, V, tqo)
            tq_g_qc = b.dma_slot(G, "odd%d" % osl,
                                 [(QT[0:64, qb + ho * 512:
                                      qb + (ho + 1) * 512],
                                   oddst[64:128, osl * 512:(osl + 1) * 512])])
            tg_toks.append(tq_g_qc)
            soddst[osl].release(G, tq_g_qc)
        b.wtd(V, ttab)
        for rc in range(2):
            psl = rc % 2
            sPM[psl].acquire(T)
            for k in range(2):
                tqr = b.op(T, lambda e, rc=rc, k=k, qof=qof, p=PM[psl]:
                           e.matmul(p[:],
                                    wQRb[:, k * 256 + rc * 128:
                                         k * 256 + (rc + 1) * 128],
                                    c_qT[:, k * S + qof: k * S + qof + 512],
                                    start=(k == 0), stop=(k == 1)))
            b.wt(V, T, tqr)
            sqr[rc].acquire(V)
            traw = b.op(V, lambda e, rc=rc, p=PM[psl]:
                        e.tensor_copy(qrraw[:, rc * 512:(rc + 1) * 512], p[:]))
            sPM[psl].release(V, traw)
            sPM[psl].acquire(T)
            b.wt(T, V, traw)
            tswp = b.op(T, lambda e, rc=rc, p=PM[psl]:
                        e.matmul(p[:], p128b[:],
                                 qrraw[:, rc * 512:(rc + 1) * 512],
                                 start=True, stop=True))
            b.wt(V, T, tswp)
            b.sw(V)
            b.op(V, lambda e, rc=rc:
                 e.tensor_tensor(tmpq[:], qrraw[:, rc * 512:(rc + 1) * 512],
                                 cosqs[:], ALU.mult))
            tr2 = b.op(V, lambda e, rc=rc, p=PM[psl]:
                       e.tensor_tensor(qrrope[:, rc * 512:(rc + 1) * 512],
                                       p[:], sinqs[:], ALU.mult))
            sPM[psl].release(V, tr2)
            b.sw(V)
            trp = b.op(V, lambda e, rc=rc:
                       e.tensor_tensor(qrrope[:, rc * 512:(rc + 1) * 512],
                                       tmpq[:],
                                       qrrope[:, rc * 512:(rc + 1) * 512],
                                       ALU.add))
            stabs[0].release(V, trp)
            b.wt(G, V, trp)
            tq_g = b.dma_slot(G, "qr%d" % rc, [
                (QT[64:96, qb + (4 * rc + h4) * 512:
                    qb + (4 * rc + h4 + 1) * 512],
                 qrrope[32 * h4:32 * h4 + 32, rc * 512:(rc + 1) * 512])
                for h4 in range(4)])
            tg_toks.append(tq_g)
            sqr[rc].release(G, tq_g)
        latest = {}
        for t in tg_toks:
            latest[t[1]] = t
        return (trp, list(latest.values()))

    qexp_tok = {0: emit_qexp(0), 1: emit_qexp(1)}

    def emit_av(d):
        b.wt(T, V, d["tclip"])
        if d["st"]:
            while any(ep["acc"] == d["acc"] for ep in pend_eps):
                step_eps(force=True)
            sdiv[d["acc"]].acquire(T)
        b.op(T, lambda e, kc=d["kcA"], h=d["h"], s2=d["sA"], w_=d["wA"],
             sl_=d["pslot"], st=d["st"], p=PA[d["acc"]]:
             e.matmul(p[0:65, s2:512], vsl(kc, h),
                      pbuf[:, sl_ * 1024: sl_ * 1024 + w_],
                      start=st, stop=False))
        tav = b.op(T, lambda e, kc=d["kcB"], h=d["h"], s2=d["sB"],
                   w_=d["wB"], sl_=d["pslot"], last=d["last"],
                   p=PA[d["acc"]]:
                   e.matmul(p[0:65, s2:512], vsl(kc, h),
                            pbuf[:, sl_ * 1024 + 512:
                                 sl_ * 1024 + 512 + w_],
                            start=False, stop=last))
        spb[d["pslot"]].release(T, tav)
        av_hist.append(tav)
        if d["last"]:
            h_, acc_, j_ = d["h"], d["acc"], d["j"]
            dsl = j_ % 2
            b.wt(SG, T, tav)
            sden[dsl].acquire(SG)
            tl1 = b.op(SG, lambda e, dd=dsl, p=PA[acc_]:
                       e.activation(den[64:65, dd * 512:(dd + 1) * 512],
                                    p[64:65, 0:512], AF.Ln))
            b.sw(SG)
            tl2 = b.op(SG, lambda e, dd=dsl:
                       e.activation(den[64:65, dd * 512:(dd + 1) * 512],
                                    den[64:65, dd * 512:(dd + 1) * 512],
                                    AF.Exp, scale=-1.0))
            b.wt(V, SG, b.cnt[SG])
            b.op(V, lambda e, dd=dsl:
                 e.tensor_copy(denb[64:65, dd * 1024:dd * 1024 + 512],
                               den[64:65, dd * 512:(dd + 1) * 512]))
            b.sw(V)
            tden = b.op(V, lambda e, dd=dsl: e.tensor_tensor(
                denb[64:65, dd * 1024 + 512:dd * 1024 + 1024],
                den[64:65, dd * 512:(dd + 1) * 512],
                denb[64:65, dd * 1024:dd * 1024 + 512], ALU.subtract))
            pend_eps.append({"h": h_, "j": j_, "acc": acc_, "dsl": dsl,
                             "pmr": j_ % 2, "tden": tden, "stage": 0,
                             "born": len(exp_hist)})

    def step_eps(force=False):
        if not pend_eps:
            return
        ep = pend_eps[0]
        if not force and len(exp_hist) - ep["born"] < 6:
            return
        if ep["stage"] == 0:
            b.wt(T, V, ep["tden"])
            sPM[ep["pmr"]].acquire(T)
            b.op(T, lambda e, d=ep["dsl"], p=PM[ep["pmr"]]:
                 e.matmul(p[0:64, :], ones64b[64:65, 0:64],
                          denb[64:65, d * 1024:d * 1024 + 512],
                          start=True, stop=False))
            ep["trep"] = b.op(T, lambda e, d=ep["dsl"], p=PM[ep["pmr"]]:
                              e.matmul(p[0:64, :], ones64b[64:65, 0:64],
                                       denb[64:65, d * 1024 + 512:
                                            d * 1024 + 1024],
                                       start=False, stop=True))
            sden[ep["dsl"]].release(T, ep["trep"])
            ep["stage"] = 1
            return
        b.wt(V, T, ep["trep"])
        srepsb[ep["pmr"]].acquire(V)
        trsb = b.op(V, lambda e, d=ep["pmr"], p=PM[ep["pmr"]]:
                    e.tensor_copy(repsb[:, d * 512:(d + 1) * 512],
                                  p[0:64, :]))
        sPM[ep["pmr"]].release(V, trsb)
        ssl = ep_ssl[0]
        ep_ssl[0] = (ep_ssl[0] + 1) % 3
        b.sw(V)
        sstage[ssl].acquire(V)
        tdiv = b.op(V, lambda e, d=ep["pmr"], sl_=ssl, p=PA[ep["acc"]]:
                    e.tensor_tensor(stage[:, sl_ * 512:(sl_ + 1) * 512],
                                    p[0:64, 0:512],
                                    repsb[:, d * 512:(d + 1) * 512],
                                    ALU.mult))
        srepsb[ep["pmr"]].release(V, tdiv)
        sdiv[ep["acc"]].release(V, tdiv)
        b.wt(G, V, tdiv)
        ccbuf = cc_inA if ep["j"] < 4 else cc_inB
        jj = ep["j"] % 4
        tcc = b.dma_slot(
            G, "st%d" % ssl,
            [(ccbuf[ep["h"] // 2, (ep["h"] % 2) * 64:
                    (ep["h"] % 2) * 64 + 64,
                    jj * 512:(jj + 1) * 512],
              stage[:, ssl * 512:(ssl + 1) * 512])])
        sstage[ssl].release(G, tcc)
        pend_eps.pop(0)

    def emit_wave(st):
        j = st["j"]
        qb = (j % 4) * HL * 512
        h, w, nw = st["h"], st["w"], st["nw"]
        acc = j % 2
        kcA, kcB = 2 * w, 2 * w + 1
        sA = 128 * max(0, kcA - 4 * j)
        sB = 128 * max(0, kcB - 4 * j)
        wA, wB = 512 - sA, 512 - sB
        gw = len(exp_hist)
        slot = gw % 2
        pslot = gw % 4
        sPS[slot].acquire(T)
        if h == 0 and w == 0:
            (tv, tgs) = qexp_tok[j]
            b.wt(T, V, tv)
            for tg in tgs:
                b.wtd(T, tg)
        b.op(T, lambda e, h=h, kc=kcA, s=sA, w_=wA, p=PS[slot], qb=qb:
             e.matmul(p[:, 0:w_],
                      K_all[0:96, h * S + kc * 128: h * S + (kc + 1) * 128],
                      QT[0:96, qb + h * 512 + s: qb + (h + 1) * 512],
                      start=True, stop=True))
        tsc = b.op(T, lambda e, h=h, kc=kcB, s=sB, w_=wB, p=PS[slot], qb=qb:
                   e.matmul(p[:, 512:512 + w_],
                            K_all[0:96, h * S + kc * 128:
                                  h * S + (kc + 1) * 128],
                            QT[0:96, qb + h * 512 + s: qb + (h + 1) * 512],
                            start=True, stop=True))
        sQTl[j % 4].release(T, tsc)
        while len(pend_av) > 2:
            emit_av(pend_av.pop(0))
        step_eps()
        b.wt(SG, T, tsc)
        if gw >= 4:
            b.wt(SG, T, av_hist_tok(gw - 4))
        spb[pslot].acquire(SG)
        texp = b.op(SG, lambda e, slot=pslot, wB=wB, p=PS[slot]:
                    e.activation(pbuf[:, slot * 1024: slot * 1024 + 512 + wB],
                                 p[:, 0:512 + wB], AF.Exp, scale=SCALE))
        sPS[slot].release(SG, texp)
        exp_hist.append(texp)
        b.wt(V, SG, texp)
        tclip = b.op(V, lambda e, slot=pslot, wB=wB:
                     e.tensor_scalar(
                         pbuf[:, slot * 1024: slot * 1024 + 512 + wB],
                         pbuf[:, slot * 1024: slot * 1024 + 512 + wB],
                         CLIP_HI, CLIP_LO, ALU.min, ALU.max))
        if kcA >= 4 * j or kcB >= 4 * j:
            b.sw(V)
        if kcA >= 4 * j:
            tclip = b.op(V, lambda e, slot=pslot:
                         e.tensor_tensor(
                             pbuf[:, slot * 1024: slot * 1024 + 128],
                             pbuf[:, slot * 1024: slot * 1024 + 128],
                             triub[:], ALU.mult))
        if kcB >= 4 * j:
            tclip = b.op(V, lambda e, slot=pslot:
                         e.tensor_tensor(
                             pbuf[:, slot * 1024 + 512:
                                  slot * 1024 + 640],
                             pbuf[:, slot * 1024 + 512:
                                  slot * 1024 + 640],
                             triub[:], ALU.mult))
        pend_av.append(dict(kcA=kcA, kcB=kcB, sA=sA, sB=sB, wA=wA, wB=wB,
                            pslot=pslot, tclip=tclip, h=h, j=j, acc=acc,
                            st=(w == 0), last=(w == nw - 1)))
        st["w"] += 1
        if st["w"] == nw:
            st["w"] = 0
            st["h"] += 1
            if st["h"] == 1 and st["j"] + 2 < NQC:
                qexp_tok[st["j"] + 2] = emit_qexp(st["j"] + 2)
        return st["h"] < HL

    for J0 in range(0, NQC, 2):
        streams = [dict(j=J0, h=0, w=0, nw=2 * J0 + 2),
                   dict(j=J0 + 1, h=0, w=0, nw=2 * J0 + 4)]
        while streams:
            for st in list(streams):
                if not emit_wave(st):
                    streams.remove(st)
        while pend_av:
            emit_av(pend_av.pop(0))
            step_eps(force=True)
        while pend_eps:
            step_eps(force=True)
        if J0 == 2:
            for kslot in range(3):
                ek = (G, "st%d" % kslot)
                if ek in b.ssem:
                    b.wtd(G, ("d", ek, b.scnt[ek]))
            tcollA[0] = b.op(G, lambda e: e.collective_compute(
                "AllGather", ALU.bypass,
                replica_groups=[[0, 1], [2, 3], [4, 5], [6, 7]],
                ins=[cc_inA.opt()], outs=[cc_outA_h.ap().opt()]), 1)

    emit_taps("attn")
    b.join()
    for kslot in range(3):
        ek = (G, "st%d" % kslot)
        if ek in b.ssem:
            b.wtd(G, ("d", ek, b.scnt[ek]))
    tcoll = b.op(G, lambda e: e.collective_compute(
        "AllGather", ALU.bypass,
        replica_groups=[[0, 1], [2, 3], [4, 5], [6, 7]],
        ins=[cc_inB.opt()], outs=[cc_outB_h.ap().opt()]), 1)

    es_a.close()
    es_b = contextlib.ExitStack()
    sbb = mk_sb(es_b)
    attnT = sbb("attnT", [128, 8 * 2048])
    woB = sbb("woB", [128, 8 * 1024])
    wffT = sbb("wffT", [128, 8 * 1024])
    wpT = sbb("wpT", [128, 8 * 256])
    wdT = sbb("wdT", [128, 8 * 15])
    wuT = sbb("wuT", [15, 1024])
    wdcat = sbb("wdcat", [15, 1024])
    simnB = sbb("simnB", [128, 2 * 4])
    simf = sbb("simf", [128, 2 * 4], F32)
    bpB = sbb("bpB", [128, 2], F32)
    yb = sbb("yb", [128, 8 * 1024], F32)
    h2T = sbb("h2T", [128, 8 * 1024])
    h2tok = sbb("h2tok", [128, 2 * 1024])
    junkB = sbb("junkB", [128, 1024], F32)
    projT = sbb("projT", [128, 2 * 512])
    projsq = sbb("projsq", [128, 512])
    retT = sbb("retT", [4, 1024])
    down15 = sbb("down15", [15, 1024])
    scaled15 = sbb("scaled15", [15, 1024])
    zsb = sbb("zsb", [128, 2 * 8 * 512])
    ffnsb = sbb("ffnsb", [128, 2 * 8 * 512])
    xresb = sbb("xresb", [128, 2 * 1024], F32)
    outsb = sbb("outsb", [128, 2 * 1024], F32)
    wstB = sbb("wstB", [128, 2 * 1024], F32)
    wcast = sbb("wcast", [128, 1024])
    lgsb = sbb("lgsb", [128, 4], F32)
    esb = sbb("esb", [128, 4], F32)
    mskb = sbb("mskb", [128, 4], F32)
    retb = sbb("retb", [128, 4])
    invT = sbb("invT", [128, 4], F32)
    invs = sbb("invs", [1, 512], F32)
    sq1 = sbb("sq1", [1, 2 * 512], F32)

    swB = [Slot(b), Slot(b)]
    sjB = Slot(b)
    sxr = [Slot(b), Slot(b)]
    sh2tok = [Slot(b), Slot(b)]
    souts = [Slot(b), Slot(b)]
    szsb = [Slot(b), Slot(b)]
    sffn = Slot(b)
    sffnb = [Slot(b), Slot(b)]
    sproj = Slot(b)
    sretT = Slot(b)
    sy = [Slot(b) for _ in range(8)]
    sstatB = [Slot(b), Slot(b)]
    sgate = Slot(b)

    def loadcast(dst, dstoff, cols, srcap, idx=[0]):
        sl = idx[0] % 2
        idx[0] += 1
        swB[sl].acquire(SY)
        td = b.dma_slot(SY, "wB%d" % sl,
                        [(wstB[:, sl * 1024: sl * 1024 + cols], srcap)])
        b.wtd(V, td)
        tc_ = b.op(V, lambda e, d=dst, o=dstoff, c=cols, s=sl:
                   e.tensor_copy(d[:, o:o + c],
                                 wstB[:, s * 1024: s * 1024 + c]))
        swB[sl].release(V, tc_)
        return tc_

    for k in range(8):
        loadcast(woB, k * 1024, 1024, wo_d[k * 128:(k + 1) * 128, :])

    def transposed_weight(src_d, nrow_chunks, dstT, cols_per_chunk):
        for r in range(nrow_chunks):
            sjB.acquire(V)
            tcast = loadcast(wcast, 0, 1024, src_d[r * 128:(r + 1) * 128, :])
            psl = r % 2
            b.wt(T, V, tcast)
            sPS[psl].acquire(T)
            for k in range(8):
                ttr = b.op(T, lambda e, k=k, p=PSB[psl]:
                           e.transpose(p[:, k * 128:(k + 1) * 128],
                                       wcast[:, k * 128:(k + 1) * 128],
                                       identB[:]))
            sjB.release(T, ttr)
            b.wt(V, T, ttr)
            dstw = dstT[:].rearrange("p (k c) -> p k c",
                                     k=8)[:, :, r * 128:(r + 1) * 128]
            srcw = PSB[psl][:, 0:1024].rearrange("p (k c) -> p k c", k=8)
            tevw = b.op(V, lambda e, d=dstw, s=srcw: e.tensor_copy(d, s))
            sPS[psl].release(V, tevw)
        return tevw

    transposed_weight(wff_d, 8, wffT, 1024)
    for pc in range(2):
        sjB.acquire(V)
        tcast = loadcast(wcast, 0, 1024, wp_d[pc * 128:(pc + 1) * 128, :])
        psl = pc % 2
        b.wt(T, V, tcast)
        sPS[psl].acquire(T)
        for k in range(8):
            ttr = b.op(T, lambda e, k=k, p=PSB[psl]:
                       e.transpose(p[:, k * 128:(k + 1) * 128],
                                   wcast[:, k * 128:(k + 1) * 128],
                                   identB[:]))
        sjB.release(T, ttr)
        b.wt(V, T, ttr)
        dstw = wpT[:].rearrange("p (k c) -> p k c",
                                k=8)[:, :, pc * 128:(pc + 1) * 128]
        srcw = PSB[psl][:, 0:1024].rearrange("p (k c) -> p k c", k=8)
        tevw = b.op(V, lambda e, d=dstw, s=srcw: e.tensor_copy(d, s))
        sPS[psl].release(V, tevw)
    swB[0].acquire(SY)
    td = b.dma(SY, wstB[0:15, 0:1024], wd_d)
    b.wt(V, SY, td)
    tcast = b.op(V, lambda e: e.tensor_copy(wdcat[:], wstB[0:15, 0:1024]))
    swB[0].release(V, tcast)
    b.wt(T, V, tcast)
    sPS[0].acquire(T)
    for k in range(8):
        ttr = b.op(T, lambda e, k=k:
                   e.transpose(PSB[0][:, k * 128:k * 128 + 15],
                               wdcat[:, k * 128:(k + 1) * 128],
                               identB[0:15, 0:15]))
    b.wt(V, T, ttr)
    for k in range(8):
        tevw = b.op(V, lambda e, k=k:
                    e.tensor_copy(wdT[:, k * 15:(k + 1) * 15],
                                  PSB[0][:, k * 128:k * 128 + 15]))
    sPS[0].release(V, tevw)
    for k in range(8):
        sl = k % 2
        swB[sl].acquire(SY)
        td = b.dma(SY, wstB[:, sl * 1024: sl * 1024 + 15],
                   wu_d[k * 128:(k + 1) * 128, :])
        b.wt(V, SY, td)
        sjB.acquire(V)
        tcast = b.op(V, lambda e, s=sl: e.tensor_copy(
            wcast[:, 0:15], wstB[:, s * 1024: s * 1024 + 15]))
        swB[sl].release(V, tcast)
        b.wt(T, V, tcast)
        sPS[1].acquire(T)
        ttr = b.op(T, lambda e: e.transpose(PSB[1][0:15, 0:128],
                                            wcast[:, 0:15], identB[:]))
        sjB.release(T, ttr)
        b.wt(V, T, ttr)
        tevw = b.op(V, lambda e, k=k:
                    e.tensor_copy(wuT[0:15, k * 128:(k + 1) * 128],
                                  PSB[1][0:15, 0:128]))
        sPS[1].release(V, tevw)
    b.dma(SY, simf[:, 0:4], sim_d[0:128, :])
    tsd_ = b.dma(SY, simf[:, 4:8], sim_d[128:256, :])
    b.wt(SG, SY, tsd_)
    sjB.acquire(SG)
    tsq = b.op(SG, lambda e: e.activation(wcast[:, 0:8], simf[:], AF.Square))
    b.wt(T, SG, tsq)
    sPS[1].acquire(T)
    for k in range(2):
        tsm = b.op(T, lambda e, k=k:
                   e.matmul(PS[1][0:1, 0:4], onesb[:],
                            wcast[:, k * 4:(k + 1) * 4],
                            start=(k == 0), stop=(k == 1)))
    sjB.release(T, tsm)
    b.wt(SG, T, tsm)
    tsq2 = b.op(SG, lambda e: e.activation(sq1[:, 0:4], PS[1][0:1, 0:4],
                                           AF.Sqrt))
    sPS[1].release(SG, tsq2)
    b.wt(V, SG, tsq2)
    tvr = b.op(V, lambda e: e.reciprocal(sq1[:, 4:8], sq1[:, 0:4]))
    b.sw(V)
    b.wt(T, V, tvr)
    sPS[1].acquire(T)
    trep = b.op(T, lambda e: e.matmul(PS[1][0:128, 0:4], ones1f[:],
                                      sq1[0:1, 4:8], start=True, stop=True))
    b.wt(V, T, trep)
    for k in range(2):
        tsimn = b.op(V, lambda e, k=k:
                     e.tensor_tensor(simnB[:, k * 4:(k + 1) * 4],
                                     simf[:, k * 4:(k + 1) * 4],
                                     PS[1][:, 0:4], ALU.mult))
    sPS[1].release(V, tsimn)
    b.dma(SY, bpB[:, 0:1], bp_d.rearrange("(p c) -> p c", c=1)[0:128])
    b.dma(SY, bpB[:, 1:2], bp_d.rearrange("(p c) -> p c", c=1)[128:256])

    roff = nc.alloc_register(mybir.EngineType.Pool, "roff")
    b.wt(G, G, tcollA[0])
    b.wt(G, G, tcoll)
    rowe = 16 * 2048
    dsemG = b.dsem[G]
    for k16 in range(16):
        bufh = cc_outA_h if k16 < 8 else cc_outB_h
        sidx, lc = (k16 % 8) // 4, (k16 % 8) % 4
        b.on(G, lambda e, k16=k16: e.reg_load(roff, hoffs[0:1, k16:k16 + 1]))
        srcap = bufh.ap()[sidx, lc]
        b.dcnt[G] += 16
        b.q[G].append((
            lambda e, srcap=srcap, ds=dsemG:
            e.dma_start(out=bass.AP(ccrb_h, roff, [[rowe, 128], [1, 2048]]),
                        in_=srcap).then_inc(ds, 16), 0, None))
    b.q[G].append((lambda e, ds=dsemG, t=b.dcnt[G]: e.wait_ge(ds, t),
                   0, None))
    b.cnt[G] += 1
    b.q[G].append((None, 1, "pub"))
    b.dma_group(G, [(attnT[:, kq * 4096:(kq + 1) * 4096],
                     ccrb_h.ap()[:, kq * 4096:(kq + 1) * 4096])
                    for kq in range(4)])
    b.join()

    sPMb = sPM
    wo_tok = {}
    h2_tok = {}
    ret_tok = {}
    s15_tok = {}
    ffn_tok = {}

    def wo_gen(sh, q2):
        c0 = sh * 1024 + q2 * 512
        zq = zsb[:, q2 * 4096:(q2 + 1) * 4096]
        szsb[q2].acquire(SG)
        for oc in range(8):
            psl = oc % 2
            sPMb[psl].acquire(T)
            for k in range(8):
                tz = b.op(T, lambda e, k=k, oc=oc, c0=c0, p=PM[psl]:
                          e.matmul(p[:],
                                   woB[:, k * 1024 + oc * 128:
                                       k * 1024 + (oc + 1) * 128],
                                   attnT[:, k * 2048 + c0:
                                         k * 2048 + c0 + 512],
                                   start=(k == 0), stop=(k == 7)))
            b.wt(SG, T, tz)
            tze = b.op(SG, lambda e, oc=oc, zq=zq, p=PM[psl]:
                       e.copy(zq[:, oc * 512:(oc + 1) * 512], p[:]))
            sPMb[psl].release(SG, tze)
            wo_tok[(sh, q2)] = tze
            if oc % 2 == 1:
                yield

    def bh1(sh, q2, ti):
        t8 = q2 * 4 + ti
        xsl = t8 % 2
        ss = xsl * 16
        zq = zsb[:, q2 * 4096:(q2 + 1) * 4096]
        sxr[xsl].acquire(SY)
        txr = b.dma_slot(SY, "xr%d" % xsl,
                         [(xresb[:, xsl * 1024:(xsl + 1) * 1024],
                           xres_d[(sh * 8 + t8) * 128:
                                  (sh * 8 + t8 + 1) * 128, :])])
        psl = ti % 2
        b.wt(T, SG, wo_tok[(sh, q2)])
        sPS[psl].acquire(T)
        for oc in range(8):
            ttz = b.op(T, lambda e, oc=oc, ti=ti, zq=zq, p=PSB[psl]:
                       e.transpose(p[:, oc * 128:(oc + 1) * 128],
                                   zq[:, oc * 512 + ti * 128:
                                      oc * 512 + (ti + 1) * 128],
                                   identB[:]))
        if ti == 3:
            szsb[q2].release(T, ttz)
        b.wt(V, T, ttz)
        b.wtd(V, txr)
        sy[t8].acquire(V)
        yt = yb[:, t8 * 1024:(t8 + 1) * 1024]
        ty = b.op(V, lambda e, yt=yt, xsl=xsl, p=PSB[psl]:
                  e.tensor_tensor(yt,
                                  xresb[:, xsl * 1024:(xsl + 1) * 1024],
                                  p[:, 0:1024], ALU.add))
        sPS[psl].release(V, ty)
        sxr[xsl].release(V, ty)
        sstatB[xsl].acquire(V)
        b.sw(V)
        b.op(V, lambda e, yt=yt, o=sc(ss + 0):
             e.reduce_sum(o, yt, axis=AX))
        tms = b.op(V, lambda e, o=sc(ss + 1): e.memset(o, 0.0))
        b.wt(SG, V, tms)
        sjB.acquire(SG)
        tsq = b.op(SG, lambda e, yt=yt, o=sc(ss + 1):
                   e.activation(junkB[:], yt, AF.Square, accum_out=o))
        return {"tsq": tsq, "psl": psl, "xsl": xsl, "ss": ss,
                "yt": yt, "t8": t8}

    def bh2(st):
        ss, xsl, psl, yt = st["ss"], st["xsl"], st["psl"], st["yt"]
        b.wt(V, SG, st["tsq"])
        b.op(V, lambda e, a=sc(ss + 0), o=sc(ss + 2):
             e.tensor_single_scalar(o, a, -1.0 / 1024.0, ALU.mult))
        b.op(V, lambda e, a=sc(ss + 1), o=sc(ss + 3):
             e.tensor_single_scalar(o, a, 1.0 / 1024.0, ALU.mult))
        b.sw(V)
        b.op(V, lambda e, a=sc(ss + 2), o=sc(ss + 4):
             e.tensor_tensor(o, a, a, ALU.mult))
        b.sw(V)
        b.op(V, lambda e, a=sc(ss + 3), c=sc(ss + 4), o=sc(ss + 5):
             e.tensor_tensor(o, a, c, ALU.subtract))
        b.sw(V)
        tv = b.op(V, lambda e, a=sc(ss + 5), o=sc(ss + 5):
                  e.tensor_single_scalar(o, a, LN_EPS, ALU.add))
        b.wt(SG, V, tv)
        b.op(SG, lambda e, a=sc(ss + 5), o=sc(ss + 6):
             e.activation(o, a, AF.Ln))
        b.sw(SG)
        b.op(SG, lambda e, a=sc(ss + 6), o=sc(ss + 7):
             e.activation(o, a, AF.Exp, scale=-0.5))
        b.sw(SG)
        b.op(SG, lambda e, m=sc(ss + 2), r=sc(ss + 7), o=sc(ss + 8):
             e.activation(o, m, AF.Identity, scale=r))
        b.sw(SG)
        sh2tok[xsl].acquire(SG)
        th2 = b.op(SG, lambda e, yt=yt, hsl=xsl, r=sc(ss + 7),
                   bi=sc(ss + 8):
                   e.activation(h2tok[:, hsl * 1024:(hsl + 1) * 1024],
                                yt, AF.Identity, bias=bi, scale=r))
        sjB.release(SG, th2)
        sstatB[xsl].release(SG, th2)
        sy[st["t8"]].release(SG, th2)
        b.wt(T, SG, th2)
        sPS[psl].acquire(T)
        for k in range(8):
            tth = b.op(T, lambda e, k=k, hsl=xsl, p=PSB[psl]:
                       e.transpose(p[:, k * 128:(k + 1) * 128],
                                   h2tok[:, hsl * 1024 + k * 128:
                                         hsl * 1024 + (k + 1) * 128],
                                   identB[:]))
        sh2tok[xsl].release(T, tth)
        b.wt(V, T, tth)
        dsth = h2T[:].rearrange("p (k t) -> p k t", k=8)[
            :, :, st["t8"] * 128:(st["t8"] + 1) * 128]
        srch = PSB[psl][:, 0:1024].rearrange("p (k t) -> p k t", k=8)
        th2e = b.op(V, lambda e, d=dsth, s2=srch: e.tensor_copy(d, s2))
        sPS[psl].release(V, th2e)
        return th2e

    def tiles_gen(sh, q2):
        pend = []
        for ti in range(5):
            if ti < 4:
                pend.append(bh1(sh, q2, ti))
            if ti >= 1:
                h2_tok[(sh, q2)] = bh2(pend.pop(0))
            yield

    def router_block(sh, q2):
        b.wt(T, V, h2_tok[(sh, q2)])
        sproj.acquire(SG)
        for pc in range(2):
            psl = pc % 2
            sPMb[psl].acquire(T)
            for k in range(8):
                tp = b.op(T, lambda e, k=k, pc=pc, q2=q2, p=PM[psl]:
                          e.matmul(p[:],
                                   wpT[:, k * 256 + pc * 128:
                                       k * 256 + (pc + 1) * 128],
                                   h2T[:, k * 1024 + q2 * 512:
                                       k * 1024 + (q2 + 1) * 512],
                                   start=(k == 0), stop=(k == 7)))
            b.wt(SG, T, tp)
            tpe = b.op(SG, lambda e, pc=pc, p=PM[psl]:
                       e.activation(projT[:, pc * 512:(pc + 1) * 512],
                                    p[:], AF.Identity,
                                    bias=bpB[:, pc:pc + 1], scale=1.0))
            sPMb[psl].release(SG, tpe)
        sPA[0].acquire(T)
        tps = 0
        for pc in range(2):
            b.sw(SG)
            if pc == 1:
                b.wt(SG, T, tps)
            tq = b.op(SG, lambda e, pc=pc:
                      e.activation(projsq[:],
                                   projT[:, pc * 512:(pc + 1) * 512],
                                   AF.Square))
            b.wt(T, SG, tq)
            tps = b.op(T, lambda e, pc=pc:
                       e.matmul(PA[0][0:1, :], onesb[:], projsq[:],
                                start=(pc == 0), stop=(pc == 1)))
        b.wt(V, T, tps)
        sl1 = sq1[:, q2 * 512:(q2 + 1) * 512]
        tmx_ = b.op(V, lambda e, a=sl1: e.tensor_single_scalar(
            a, PA[0][0:1, :], 1e-24, ALU.max))
        sPA[0].release(V, tmx_)
        b.wt(SG, V, tmx_)
        b.op(SG, lambda e, a=sl1: e.activation(a, a, AF.Ln))
        b.sw(SG)
        tex_ = b.op(SG, lambda e, a=sl1:
                    e.activation(a, a, AF.Exp, scale=-0.5))
        b.wt(V, SG, tex_)
        tinv = b.op(V, lambda e, a=sl1: e.tensor_single_scalar(
            invs[:], a, scaleT[0:1, 0:1], ALU.mult))
        b.wt(T, V, tinv)
        sPA[1].acquire(T)
        for ti in range(4):
            tti = b.op(T, lambda e, ti=ti:
                       e.transpose(PA[1][0:128, ti:ti + 1],
                                   invs[0:1, ti * 128:(ti + 1) * 128],
                                   identF[0:1, 0:1]))
        b.wt(V, T, tti)
        sgate.acquire(V)
        tiv = b.op(V, lambda e: e.tensor_copy(invT[:], PA[1][:, 0:4]))
        sPA[1].release(V, tiv)
        for ti in range(4):
            sPA[1].acquire(T)
            b.wt(T, V, tiv)
            for pc in range(2):
                tl = b.op(T, lambda e, pc=pc, ti=ti, q2=q2:
                          e.matmul(PA[1][0:128, 0:4],
                                   projT[:, pc * 512 + ti * 128:
                                         pc * 512 + (ti + 1) * 128],
                                   simnB[:, pc * 4:(pc + 1) * 4],
                                   start=(pc == 0), stop=(pc == 1)))
            if ti == 3:
                sproj.release(T, tl)
            b.wt(V, T, tl)
            tlg = b.op(V, lambda e, ti=ti:
                       e.tensor_single_scalar(lgsb[:], PA[1][:, 0:4],
                                              invT[:, ti:ti + 1],
                                              ALU.mult))
            sPA[1].release(V, tlg)
            b.sw(V)
            b.op(V, lambda e: e.reduce_max(sc(9), lgsb[:], axis=AX))
            tmx = b.op(V, lambda e: e.reduce_max(sc(10), lgsb[:], axis=AX,
                                                 negate=True))
            b.wt(SG, V, tmx)
            te = b.op(SG, lambda e: e.activation(esb[:], lgsb[:], AF.Exp,
                                                 bias=sc(10), scale=1.0))
            b.wt(V, SG, te)
            b.op(V, lambda e: e.reduce_sum(sc(11), esb[:], axis=AX))
            b.sw(V)
            b.op(V, lambda e: e.reciprocal(sc(12), sc(11)))
            b.op(V, lambda e: e.tensor_single_scalar(mskb[:], lgsb[:],
                                                     sc(9), ALU.is_equal))
            b.sw(V)
            b.op(V, lambda e: e.tensor_tensor(esb[:], mskb[:], esb[:],
                                              ALU.mult))
            b.sw(V)
            trt = b.op(V, lambda e:
                       e.tensor_single_scalar(retb[:], esb[:], sc(12),
                                              ALU.mult))
            b.wt(T, V, trt)
            sPA[1].acquire(T)
            ttr = b.op(T, lambda e: e.transpose(PA1B[0:4, 0:128],
                                                retb[:], identB[:]))
            b.wt(V, T, ttr)
            if q2 == 0 and ti == 0:
                sretT.acquire(V)
            trc = b.op(V, lambda e, q2=q2, ti=ti: e.tensor_copy(
                retT[0:4, q2 * 512 + ti * 128:
                     q2 * 512 + (ti + 1) * 128],
                PA1B[0:4, 0:128]))
            sPA[1].release(V, trc)
        sgate.release(V, trc)
        ret_tok[(sh, q2)] = trc

    def dn_block(sh, q2):
        sPA[0].acquire(T)
        b.wt(T, V, h2_tok[(sh, q2)])
        for k in range(8):
            tdn = b.op(T, lambda e, k=k, q2=q2:
                       e.matmul(PA[0][0:15, :],
                                wdT[:, k * 15:(k + 1) * 15],
                                h2T[:, k * 1024 + q2 * 512:
                                    k * 1024 + (q2 + 1) * 512],
                                start=(k == 0), stop=(k == 7)))
        b.wt(V, T, tdn)
        if q2 == 0:
            sffn.acquire(V)
        tdv = b.op(V, lambda e, q2=q2:
                   e.tensor_copy(down15[0:15, q2 * 512:(q2 + 1) * 512],
                                 PA[0][0:15, :]))
        sPA[0].release(V, tdv)
        sPA[0].acquire(T)
        b.wt(T, V, ret_tok[(sh, q2)])
        trp = b.op(T, lambda e, q2=q2:
                   e.matmul(PA[0][0:15, :], e415b[:],
                            retT[0:4, q2 * 512:(q2 + 1) * 512],
                            start=True, stop=True))
        if q2 == 1:
            sretT.release(T, trp)
        b.wt(V, T, trp)
        b.sw(V)
        tsc15 = b.op(V, lambda e, q2=q2: e.tensor_tensor(
            scaled15[0:15, q2 * 512:(q2 + 1) * 512],
            down15[0:15, q2 * 512:(q2 + 1) * 512],
            PA[0][0:15, :], ALU.mult))
        sPA[0].release(V, tsc15)
        s15_tok[(sh, q2)] = tsc15

    def wff_gen(sh, q2):
        fq = ffnsb[:, q2 * 4096:(q2 + 1) * 4096]
        b.wt(T, V, s15_tok[(sh, q2)])
        for oc in range(8):
            psl = oc % 2
            sPMb[psl].acquire(T)
            for k in range(8):
                b.op(T, lambda e, k=k, oc=oc, q2=q2, p=PM[psl]:
                     e.matmul(p[:],
                              wffT[:, k * 1024 + oc * 128:
                                   k * 1024 + (oc + 1) * 128],
                              h2T[:, k * 1024 + q2 * 512:
                                  k * 1024 + (q2 + 1) * 512],
                              start=(k == 0), stop=False))
            tf = b.op(T, lambda e, oc=oc, q2=q2, p=PM[psl]:
                      e.matmul(p[:], wuT[0:15, oc * 128:(oc + 1) * 128],
                               scaled15[0:15, q2 * 512:(q2 + 1) * 512],
                               start=False, stop=True))
            if oc == 7 and q2 == 1:
                sffn.release(T, tf)
            b.wt(SG, T, tf)
            if oc == 0:
                sffnb[q2].acquire(SG)
            tfe = b.op(SG, lambda e, oc=oc, fq=fq, p=PM[psl]:
                       e.copy(fq[:, oc * 512:(oc + 1) * 512], p[:]))
            sPMb[psl].release(SG, tfe)
            ffn_tok[(sh, q2)] = tfe
            yield

    def out_gen(sh, q2):
        fq = ffnsb[:, q2 * 4096:(q2 + 1) * 4096]
        for ti in range(4):
            t8 = q2 * 4 + ti
            psl = ti % 2
            osl = t8 % 2
            b.wt(T, SG, ffn_tok[(sh, q2)])
            sPS[psl].acquire(T)
            for oc in range(8):
                tto = b.op(T, lambda e, oc=oc, ti=ti, fq=fq, p=PSB[psl]:
                           e.transpose(p[:, oc * 128:(oc + 1) * 128],
                                       fq[:, oc * 512 + ti * 128:
                                          oc * 512 + (ti + 1) * 128],
                                       identB[:]))
            if ti == 3:
                sffnb[q2].release(T, tto)
            b.wt(V, T, tto)
            souts[osl].acquire(V)
            tout = b.op(V, lambda e, t8=t8, osl=osl, p=PSB[psl]:
                        e.tensor_tensor(
                            outsb[:, osl * 1024:(osl + 1) * 1024],
                            yb[:, t8 * 1024:(t8 + 1) * 1024],
                            p[:, 0:1024], ALU.add))
            sPS[psl].release(V, tout)
            sy[t8].release(V, tout)
            b.wt(G, V, tout)
            tod = b.dma_slot(G, "out%d" % osl,
                             [(o_d[(sh * 8 + t8) * 128:
                                   (sh * 8 + t8 + 1) * 128, :],
                               outsb[:, osl * 1024:(osl + 1) * 1024])])
            souts[osl].release(G, tod)
            yield

    def run_gen(g):
        for _ in g:
            pass

    def interleave(a, bgen):
        while True:
            done = 0
            for g in (a, bgen):
                try:
                    next(g)
                except StopIteration:
                    done += 1
            if done == 2:
                break

    prev_out = None
    for sh in range(2):
        if prev_out is None:
            run_gen(wo_gen(sh, 0))
        else:
            interleave(prev_out, wo_gen(sh, 0))
        interleave(tiles_gen(sh, 0), wo_gen(sh, 1))
        router_block(sh, 0)
        dn_block(sh, 0)
        interleave(tiles_gen(sh, 1), wff_gen(sh, 0))
        router_block(sh, 1)
        dn_block(sh, 1)
        interleave(out_gen(sh, 0), wff_gen(sh, 1))
        prev_out = out_gen(sh, 1)
    run_gen(prev_out)

    emit_taps("b")

    b.wt(G, G, b.cnt[G])
    for ek, v in list(b.scnt.items()):
        if v > 0:
            b.wtd(G, ("d", ek, v))

    with nc.Block() as block:
        b.finalize(block)
    es_b.close()
    es.close()
    return nc


_NC_CACHE = {}


def _make_consts():
    inv = np.power(np.float64(10000.0),
                   -np.arange(0, 32, 2, dtype=np.float64) / 32.0)
    t = np.arange(S, dtype=np.float64)
    ang = t[None, :] * inv[:, None]
    cos2 = np.repeat(np.cos(ang), 2, axis=0)
    sin_ = np.sin(ang)
    sin2 = np.empty((32, S), np.float64)
    sin2[0::2] = -sin_
    sin2[1::2] = sin_
    cosq = np.tile(cos2, (4, 1)).astype(np.float32)
    sinq = np.tile(sin2, (4, 1)).astype(np.float32)
    cosk = np.empty((128, 1024), np.float32)
    sink = np.empty((128, 1024), np.float32)
    for ch in range(4):
        cosk[32 * ch:32 * ch + 32] = cos2[:, ch * 1024:(ch + 1) * 1024]
        sink[32 * ch:32 * ch + 32] = sin2[:, ch * 1024:(ch + 1) * 1024]
    ident = np.eye(128, dtype=np.float32)
    p128 = np.zeros((128, 128), np.float32)
    for i in range(0, 128, 2):
        p128[i + 1, i] = 1.0
        p128[i, i + 1] = 1.0
    triu = np.triu(np.ones((128, 128), np.float32))
    e415 = np.zeros((4, 15), np.float32)
    rows = [0, 1, 1, 2, 2, 2, 2, 3, 3, 3, 3, 3, 3, 3, 3]
    for jj, ee in enumerate(rows):
        e415[ee, jj] = 1.0
    return dict(cosq=cosq, sinq=sinq, cosk=cosk, sink=sink, ident=ident,
                p128=p128, triu=triu, e415=e415)


def kernel(**inputs):
    if "nc" not in _NC_CACHE:
        _NC_CACHE["nc"] = build_program()
    nc = _NC_CACHE["nc"]
    C = _make_consts()
    asc = np.ascontiguousarray

    def g(k):
        return np.asarray(inputs[k], dtype=np.float32)

    x = g("L0_attn_x")
    wd = np.concatenate([g("L0_ffn_Wd0"), g("L0_ffn_Wd1"),
                         g("L0_ffn_Wd2"), g("L0_ffn_Wd3")], axis=0)
    wu = np.concatenate([g("L0_ffn_Wu0"), g("L0_ffn_Wu1"),
                         g("L0_ffn_Wu2"), g("L0_ffn_Wu3")], axis=1)
    in_maps = []
    for c in range(8):
        bb, hh = c // 2, c % 2
        hoff = np.zeros((1, 16), np.int32)
        for k16 in range(16):
            g8 = k16 % 8
            from_a = k16 < 8
            mine = (hh == 0) == from_a
            hoff[0, k16] = g8 * 2048 if mine else (8 + g8) * 2048
        in_maps.append(dict(
            x=asc(x[bb]),
            xres=asc(x[bb, hh * 2048:(hh + 1) * 2048]),
            wdq=g("L0_attn_W_DQ"), wdkv=g("L0_attn_W_DKV"),
            wkr=g("L0_attn_W_KR"),
            wuq=asc(g("L0_attn_W_UQ")[:, hh * 512:(hh + 1) * 512]),
            wqr=asc(g("L0_attn_W_QR")[:, hh * 256:(hh + 1) * 256]),
            wuk=asc(g("L0_attn_W_UK")[:, hh * 512:(hh + 1) * 512]),
            wuv=asc(g("L0_attn_W_UV")[:, hh * 512:(hh + 1) * 512]),
            wo=g("L0_attn_W_O"), wff=g("L0_ffn_W"), wp=g("L0_ffn_Wp"),
            bp=g("L0_ffn_bp"), sim=g("L0_ffn_sim"), temp=g("L0_ffn_temp"),
            wd=wd, wu=wu, hoff=hoff, **C,
        ))
    res = run_bass_kernel_spmd(nc, in_maps, core_ids=list(range(8)))
    _NC_CACHE["res"] = res
    out = np.empty((4, S, D), np.float32)
    for c in range(8):
        bb, hh = c // 2, c % 2
        out[bb, hh * 2048:(hh + 1) * 2048] = res.results[c]["o"]
    return out


# version 23 (final)
# speedup vs baseline: 1.0079x; 1.0079x over previous
import contextlib

import numpy as np

import concourse.bass as bass
import concourse.mybir as mybir
from concourse.bass_utils import run_bass_kernel_spmd

F32 = mybir.dt.float32
BF16 = mybir.dt.bfloat16
I32 = mybir.dt.int32
AF = mybir.ActivationFunctionType
ALU = mybir.AluOpType
AX = mybir.AxisListType.X

S = 4096
D = 1024
HL = 8
DL = 256
NKC = S // 128
NQC = S // 512
SCALE = float(1.0 / np.sqrt(np.float64(96.0)))
CLIP_LO = float(np.exp(np.float64(-80.0) / np.sqrt(np.float64(96.0))))
CLIP_HI = float(np.exp(np.float64(80.0) / np.sqrt(np.float64(96.0))))
LN_EPS = 1e-5
CLAMP_MAX = float(np.log(np.float64(100.0)))

ENGS = ("tensor", "vector", "scalar", "sync", "gpsimd")
T, V, SG, SY, G = ENGS

DEBUG_TAPS = []


class B:

    def __init__(self, nc, es):
        self.nc = nc
        self.q = {e: [] for e in ENGS}
        self.sem = {e: es.enter_context(nc.semaphore("ch_" + e)) for e in ENGS}
        self.cnt = {e: 0 for e in ENGS}
        self.used = {e: set() for e in ENGS}
        self.dsem = {e: es.enter_context(nc.semaphore("dma_" + e))
                     for e in ("sync", "gpsimd")}
        self.dcnt = {e: 0 for e in ("sync", "gpsimd")}
        self.es = es
        self.ssem = {}
        self.scnt = {}

    def on(self, eng, fn):
        self.q[eng].append((fn, 0, None))

    def op(self, eng, fn, step=1):
        self.cnt[eng] += step
        self.q[eng].append((fn, step, "op"))
        return self.cnt[eng]

    def wt(self, eng, src, tok):
        assert tok <= self.cnt[src], (eng, src, tok, self.cnt[src])
        if tok <= 0:
            return
        self.used[src].add(tok)
        sem = self.sem[src]
        self.q[eng].append((lambda e, s=sem, v=tok: e.wait_ge(s, v), 0, None))

    def dma(self, eng, out, in_):
        return self.dma_group(eng, [(out, in_)])

    def dma_group(self, eng, pairs):
        dsem = self.dsem[eng]
        for (out, in_) in pairs:
            self.dcnt[eng] += 16
            self.q[eng].append((
                lambda e, o=out, i=in_, ds=dsem:
                e.dma_start(out=o, in_=i).then_inc(ds, 16), 0, None))
        dtarget = self.dcnt[eng]
        self.q[eng].append((lambda e, ds=dsem, t=dtarget: e.wait_ge(ds, t),
                            0, None))
        self.cnt[eng] += 1
        self.q[eng].append((None, 1, "pub"))
        return self.cnt[eng]

    def dma_slot(self, eng, key, pairs):
        if (eng, key) not in self.ssem:
            self.ssem[(eng, key)] = self.es.enter_context(
                self.nc.semaphore("sd_%s_%s" % (eng, key)))
            self.scnt[(eng, key)] = 0
        sem = self.ssem[(eng, key)]
        for (out, in_) in pairs:
            self.scnt[(eng, key)] += 16
            self.q[eng].append((
                lambda e, o=out, i=in_, ds=sem:
                e.dma_start(out=o, in_=i).then_inc(ds, 16), 0, None))
        return ("d", (eng, key), self.scnt[(eng, key)])

    def wtd(self, eng, tok):
        _, ek, val = tok
        sem = self.ssem[ek]
        self.q[eng].append((lambda e, s=sem, v=val: e.wait_ge(s, v), 0, None))

    def sw(self, eng):
        self.wt(eng, eng, self.cnt[eng])

    def join(self):
        snap = dict(self.cnt)
        ssnap = dict(self.scnt)
        for e in ENGS:
            for o in ENGS:
                if o != e and snap[o] > 0:
                    self.wt(e, o, snap[o])
            for ek, v in ssnap.items():
                if v > 0:
                    self.wtd(e, ("d", ek, v))

    def finalize(self, block):
        for eng in ENGS:
            queue, sem = self.q[eng], self.sem[eng]
            used = self.used[eng]
            plan = []
            val = 0
            pending = 0
            for (fn, step, kind) in queue:
                val += step
                pending += step
                if step > 0 and val in used:
                    plan.append((fn, pending, kind))
                    pending = 0
                elif kind == "pub":
                    plan.append((None, 0, None))
                else:
                    plan.append((fn, 0, kind))

            def emit(e, plan=plan, sem=sem):
                for fn, inc, kind in plan:
                    if kind == "pub":
                        e.sem_inc(sem, inc)
                        continue
                    if fn is None:
                        continue
                    ins = fn(e)
                    if inc:
                        ins.then_inc(sem, inc)

            getattr(block, eng)(emit)


class Slot:

    def __init__(self, b):
        self.b = b
        self.guards = []

    def acquire(self, writer_eng):
        for (e, t) in self.guards:
            if isinstance(t, tuple):
                self.b.wtd(writer_eng, t)
            elif e != writer_eng:
                self.b.wt(writer_eng, e, t)
        self.guards = []

    def release(self, eng, tok):
        for i, (e, t) in enumerate(self.guards):
            if e == eng and not isinstance(t, tuple) and not isinstance(tok, tuple):
                self.guards[i] = (e, max(t, tok))
                return
        self.guards.append((eng, tok))


def build_program():
    nc = bass.Bass("TRN2", target_bir_lowering=False)

    def dt(n, shp, ty=F32, kind="ExternalInput"):
        return nc.dram_tensor(n, shp, ty, kind=kind)

    x_d = dt("x", [S, D]).ap()
    xres_d = dt("xres", [S // 2, D]).ap()
    wdq_d = dt("wdq", [D, DL]).ap()
    wdkv_d = dt("wdkv", [D, DL]).ap()
    wkr_d = dt("wkr", [D, 32]).ap()
    wuq_d = dt("wuq", [DL, 512]).ap()
    wqr_d = dt("wqr", [DL, 256]).ap()
    wuk_d = dt("wuk", [DL, 512]).ap()
    wuv_d = dt("wuv", [DL, 512]).ap()
    wo_d = dt("wo", [D, D]).ap()
    wff_d = dt("wff", [D, D]).ap()
    wp_d = dt("wp", [DL, D]).ap()
    bp_d = dt("bp", [DL]).ap()
    sim_d = dt("sim", [DL, 4]).ap()
    temp_d = dt("temp", [1]).ap()
    wd_d = dt("wd", [15, D]).ap()
    wu_d = dt("wu", [D, 15]).ap()
    cosq_d = dt("cosq", [128, S]).ap()
    sinq_d = dt("sinq", [128, S]).ap()
    cosk_d = dt("cosk", [128, 1024]).ap()
    sink_d = dt("sink", [128, 1024]).ap()
    ident_d = dt("ident", [128, 128]).ap()
    p128_d = dt("p128", [128, 128]).ap()
    triu_d = dt("triu", [128, 128]).ap()
    e415_d = dt("e415", [4, 15]).ap()
    hoff_d = dt("hoff", [1, 16], I32).ap()
    o_d = dt("o", [S // 2, D], F32, kind="ExternalOutput").ap()
    cc_inA_h = dt("cc_inA", [4, 128, S // 2], BF16, kind="Internal")
    cc_inB_h = dt("cc_inB", [4, 128, S // 2], BF16, kind="Internal")
    cc_outA_h = dt("cc_outA", [2, 4, 128, S // 2], BF16, kind="Internal")
    cc_outB_h = dt("cc_outB", [2, 4, 128, S // 2], BF16, kind="Internal")
    cc_inA = cc_inA_h.ap()
    cc_inB = cc_inB_h.ap()
    ccrb_h = dt("ccrb", [128, 16 * 2048], BF16, kind="Internal")
    dbg_d = {}
    for tap in DEBUG_TAPS:
        name, _, (p0, p1), (c0, c1), ty, phase = tap
        dbg_d[name] = dt("dbg_" + name, [p1 - p0, c1 - c0], ty,
                         kind="ExternalOutput").ap()

    es = contextlib.ExitStack()
    b = B(nc, es)
    sbuf = {}

    def emit_taps(phase):
        if not DEBUG_TAPS:
            return
        todo = [t for t in DEBUG_TAPS if t[5] == phase]
        if not todo:
            return
        b.join()
        for (name, sbn, (p0, p1), (c0, c1), ty, _) in todo:
            b.dma(G, dbg_d[name], sbuf[sbn][p0:p1, c0:c1])
        b.join()

    def mk_sb(stack):
        def f(n, shp, ty=BF16):
            t = stack.enter_context(nc.sbuf_tensor(n, shp, ty))
            sbuf[n] = t
            return t
        return f

    def psum(n, shp):
        return es.enter_context(nc.psum_tensor(n, shp, F32))

    PS = [psum("PS0", [128, 1024]), psum("PS1", [128, 1024])]
    PM = [psum("PM0", [128, 512]), psum("PM1", [128, 512])]
    PA = [psum("PA0", [128, 512]), psum("PA1", [128, 512])]
    PSB = [PS[0][:].bitcast(BF16), PS[1][:].bitcast(BF16)]
    PMB = [PM[0][:].bitcast(BF16), PM[1][:].bitcast(BF16)]
    PA1B = PA[1][:].bitcast(BF16)
    sPS = [Slot(b), Slot(b)]
    sPM = [Slot(b), Slot(b)]
    sPA = [Slot(b), Slot(b)]

    sbr = mk_sb(es)
    identF = sbr("identF", [128, 128], F32)
    identB = sbr("identB", [128, 128])
    p128b = sbr("p128b", [128, 128])
    triub = sbr("triub", [128, 128])
    e415b = sbr("e415b", [4, 15])
    onesb = sbr("onesb", [128, 1])
    ones64f = sbr("ones64f", [65, 64], F32)
    ones64b = sbr("ones64b", [65, 64])
    ones1f = sbr("ones1f", [1, 128], F32)
    hoffs = sbr("hoffs", [1, 16], I32)
    scaleT = sbr("scaleT", [1, 1], F32)
    smalls = sbr("smalls", [128, 64], F32)

    def sc(i):
        return smalls[:, i:i + 1]

    es_a = contextlib.ExitStack()
    sba = mk_sb(es_a)
    c_kvT = sba("c_kvT", [128, 2 * S])
    c_qT = sba("c_qT", [128, 2 * S])
    krTb = sba("krTb", [128, 1024])
    wUQb = sba("wUQb", [128, 2 * 512])
    wQRb = sba("wQRb", [128, 2 * 256])
    wUKb = sba("wUKb", [128, 2 * 512])
    wUVb = sba("wUVb", [128, 2 * 512])
    QT = sba("QT", [128, 4 * HL * 512])
    pbuf = sba("pbuf", [128, 4 * 1024])
    stage = sba("stage", [64, 3 * 512])
    oddst = sba("oddst", [128, 2 * 512])
    qrraw = sba("qrraw", [128, 2 * 512])
    qrrope = sba("qrrope", [128, 2 * 512])
    tmpq = sba("tmpq", [128, 512], F32)
    den = sba("den", [65, 2 * 512], F32)
    denb = sba("denb", [65, 2 * 1024])
    repsb = sba("repsb", [64, 2 * 512], F32)
    cosqs = sba("cosqs", [128, 512], F32)
    sinqs = sba("sinqs", [128, 512], F32)

    sQTl = [Slot(b) for _ in range(4)]
    spb = [Slot(b), Slot(b), Slot(b), Slot(b)]
    sstage = [Slot(b) for _ in range(3)]
    soddst = [Slot(b), Slot(b)]
    sden = [Slot(b), Slot(b)]
    srepsb = [Slot(b), Slot(b)]
    stabs = [Slot(b)]
    sqr = [Slot(b), Slot(b)]

    es_a0 = contextlib.ExitStack()
    sba0 = mk_sb(es_a0)
    wstage = sba0("wstage", [128, 4 * 2048], F32)
    wDQb = sba0("wDQb", [128, 8 * DL])
    wDKVb = sba0("wDKVb", [128, 8 * DL])
    wKRb = sba0("wKRb", [128, 8 * 32])
    xst = sba0("xst", [128, 4 * 1024], F32)
    htok = sba0("htok", [128, 4 * 1024])
    junkA = sba0("junkA", [128, 1024], F32)
    hT = sba0("hT", [128, 2 * 4096])
    krTraw = sba0("krTraw", [128, 1024])
    krstage = sba0("krstage", [32, 2 * 512])
    coskA = sba0("coskA", [128, 1024], F32)
    sinkA = sba0("sinkA", [128, 1024], F32)
    swst = [Slot(b) for _ in range(4)]
    sxst = [Slot(b) for _ in range(4)]
    shtok = [Slot(b) for _ in range(4)]
    shT = [Slot(b), Slot(b)]
    skrst = [Slot(b), Slot(b)]
    sstat = [Slot(b) for _ in range(4)]

    b.dma(SY, identF[:], ident_d)
    b.dma(SY, wstage[:, 0:128], p128_d)
    b.dma(SY, wstage[:, 128:256], triu_d)
    tcd = b.dma(SY, wstage[0:4, 256:271], e415_d)
    b.wt(V, SY, tcd)
    b.op(V, lambda e: e.tensor_copy(identB[:], identF[:]))
    b.op(V, lambda e: e.tensor_copy(p128b[:], wstage[:, 0:128]))
    b.op(V, lambda e: e.tensor_copy(triub[:], wstage[:, 128:256]))
    b.op(V, lambda e: e.tensor_copy(e415b[:], wstage[0:4, 256:271]))
    b.op(V, lambda e: e.memset(onesb[:], 1.0))
    b.op(V, lambda e: e.memset(ones64f[:], 0.0))
    b.sw(V)
    b.op(V, lambda e: e.memset(ones64f[64:65, :], 1.0))
    b.op(V, lambda e: e.memset(ones64b[:], 0.0))
    b.sw(V)
    b.op(V, lambda e: e.memset(ones64b[64:65, :], 1.0))
    tcv = b.op(V, lambda e: e.memset(ones1f[:], 1.0))
    for _sw in swst:
        _sw.release(V, tcv)

    wtok = {}
    wpend = {}

    def weight_issue(widx):
        name, src, dst, rows, cols = wq[widx]
        kk = rows // 128
        sl = widx % 4
        swst[sl].acquire(G)
        for k0 in range(0, kk, 2):
            k1 = min(kk, k0 + 2)
            td = b.dma_slot(
                G, "wst%d" % sl,
                [(wstage[:, sl * 2048 + k0 * cols:
                         sl * 2048 + k1 * cols].rearrange(
                             "p (k c) -> p k c", k=k1 - k0),
                  src[k0 * 128:k1 * 128, :].rearrange("(k p) c -> p k c",
                                                      p=128))])
        wpend[name] = (td, sl)

    def weight_cast(widx):
        name, src, dst, rows, cols = wq[widx]
        tot = (rows // 128) * cols
        td, sl = wpend[name]
        b.wtd(V, td)
        for c0 in range(0, tot, 1024):
            c1 = min(tot, c0 + 1024)
            tc_ = b.op(V, lambda e, sl=sl, d=dst, c0=c0, c1=c1:
                       e.tensor_copy(d[:, c0:c1],
                                     wstage[:, sl * 2048 + c0:
                                            sl * 2048 + c1]))
        swst[sl].release(V, tc_)
        wtok[name] = tc_

    tt0 = b.dma(SY, scaleT[:], temp_d.rearrange("(a c) -> a c", a=1))
    b.wt(V, SY, tt0)
    tt1 = b.op(V, lambda e: e.tensor_single_scalar(scaleT[:], scaleT[:],
                                                   CLAMP_MAX, ALU.min))
    b.wt(SG, V, tt1)
    b.op(SG, lambda e: e.activation(scaleT[:], scaleT[:], AF.Exp))

    tok_kr_dma = []
    st1 = {}
    st3 = {}
    ev_v = {}
    ev_s = {}

    def a0_stage1(i):
        xsl = i % 4
        ss = xsl * 16
        sxst[xsl].acquire(SY)
        tx = b.dma_slot(SY, "x%d" % xsl,
                        [(xst[:, xsl * 1024:(xsl + 1) * 1024],
                          x_d[i * 128:(i + 1) * 128, :])])
        xt = xst[:, xsl * 1024:(xsl + 1) * 1024]
        b.wtd(V, tx)
        sstat[xsl].acquire(V)
        b.op(V, lambda e, xt=xt, o=sc(ss + 0): e.reduce_sum(o, xt, axis=AX))
        tms = b.op(V, lambda e, o=sc(ss + 1): e.memset(o, 0.0))
        b.wtd(SG, tx)
        b.wt(SG, V, tms)
        tsq = b.op(SG, lambda e, xt=xt, o=sc(ss + 1):
                   e.activation(junkA[:], xt, AF.Square, accum_out=o))
        st1[i] = tsq

    def a0_stage2(i):
        xsl = i % 4
        ss = xsl * 16
        b.wt(V, SG, st1[i])
        b.op(V, lambda e, a=sc(ss + 0), o=sc(ss + 2):
             e.tensor_single_scalar(o, a, -1.0 / 1024.0, ALU.mult))
        b.op(V, lambda e, a=sc(ss + 1), o=sc(ss + 3):
             e.tensor_single_scalar(o, a, 1.0 / 1024.0, ALU.mult))
        b.sw(V)
        b.op(V, lambda e, a=sc(ss + 2), o=sc(ss + 4):
             e.tensor_tensor(o, a, a, ALU.mult))
        b.sw(V)
        b.op(V, lambda e, a=sc(ss + 3), c=sc(ss + 4), o=sc(ss + 5):
             e.tensor_tensor(o, a, c, ALU.subtract))
        b.sw(V)
        tv = b.op(V, lambda e, a=sc(ss + 5), o=sc(ss + 5):
                  e.tensor_single_scalar(o, a, LN_EPS, ALU.add))
        b.wt(SG, V, tv)
        b.op(SG, lambda e, a=sc(ss + 5), o=sc(ss + 6):
             e.activation(o, a, AF.Ln))
        b.sw(SG)
        b.op(SG, lambda e, a=sc(ss + 6), o=sc(ss + 7):
             e.activation(o, a, AF.Exp, scale=-0.5))
        b.sw(SG)
        st1[i] = b.op(SG, lambda e, m=sc(ss + 2), r=sc(ss + 7),
                      o=sc(ss + 8):
                      e.activation(o, m, AF.Identity, scale=r))

    def a0_stage3(i):
        xsl = i % 4
        ss = xsl * 16
        xt = xst[:, xsl * 1024:(xsl + 1) * 1024]
        ht = htok[:, xsl * 1024:(xsl + 1) * 1024]
        b.sw(SG)
        shtok[xsl].acquire(SG)
        tap_ = b.op(SG, lambda e, xt=xt, ht=ht, r=sc(ss + 7), bi=sc(ss + 8):
                    e.activation(ht, xt, AF.Identity, bias=bi, scale=r))
        sxst[xsl].release(SG, tap_)
        sstat[xsl].release(SG, tap_)
        psl = i % 2
        b.wt(T, SG, tap_)
        sPS[psl].acquire(T)
        for k in range(8):
            ttr = b.op(T, lambda e, ht=ht, k=k, p=PSB[psl]:
                       e.transpose(p[:, k * 128:(k + 1) * 128],
                                   ht[:, k * 128:(k + 1) * 128], identB[:]))
        shtok[xsl].release(T, ttr)
        st3[i] = ttr

    def a0_stage4(i):
        i4 = i % 4
        tc = i // 4
        hsl = tc % 2
        psl = i % 2
        b.wt(V, T, st3[i])
        if i4 == 0:
            shT[hsl].acquire(V)
        dst = hT[:, hsl * 4096: (hsl + 1) * 4096].rearrange(
            "p (k t) -> p k t", k=8)[:, :, i4 * 128:(i4 + 1) * 128]
        src = PSB[psl][:, 0:1024].rearrange("p (k t) -> p k t", k=8)
        tev = b.op(V, lambda e, d=dst, s=src: e.tensor_copy(d, s))
        ev_v[tc] = tev
        sPS[psl].release(V, tev)

    proj_tok = {}

    def a0_proj_mm(tc, part):
        hsl = tc % 2
        hTc = hT[:, hsl * 4096: hsl * 4096 + 4096]
        b.wt(T, V, ev_v[tc])
        if part < 2:
            (nm, wsrc) = (("wdkv", wDKVb), ("wdq", wDQb))[part]
            tg = (sPM if part == 0 else sPA)
            pg = (PM if part == 0 else PA)
            b.wt(T, V, wtok[nm])
            for mc in range(2):
                tg[mc].acquire(T)
                for k in range(8):
                    tmm = b.op(T, lambda e, w=wsrc, mc=mc, k=k, h=hTc,
                               p=pg[mc]:
                               e.matmul(p[:],
                                        w[:, k * 256 + mc * 128:
                                          k * 256 + (mc + 1) * 128],
                                        h[:, k * 512:(k + 1) * 512],
                                        start=(k == 0), stop=(k == 7)))
                shT[hsl].release(T, tmm)
            proj_tok[(tc, part)] = tmm
        else:
            b.wt(T, V, wtok["wkr"])
            sPM[0].acquire(T)
            for k in range(8):
                tkr = b.op(T, lambda e, k=k, h=hTc:
                           e.matmul(PM[0][0:32, :],
                                    wKRb[:, k * 32:(k + 1) * 32],
                                    h[:, k * 512:(k + 1) * 512],
                                    start=(k == 0), stop=(k == 7)))
            shT[hsl].release(T, tkr)
            proj_tok[(tc, part)] = tkr

    def a0_proj_ev(tc, part):
        b.wt(V, T, proj_tok[(tc, part)])
        if part < 2:
            dstt = (c_kvT, c_qT)[part]
            tg = (sPM if part == 0 else sPA)
            pg = (PM if part == 0 else PA)
            for mc in range(2):
                tevc = b.op(V, lambda e, d=dstt, mc=mc, tc=tc, p=pg[mc]:
                            e.tensor_copy(
                                d[:, mc * S + tc * 512:
                                  mc * S + (tc + 1) * 512], p[:]))
                tg[mc].release(V, tevc)
        else:
            ksl = tc % 2
            skrst[ksl].acquire(V)
            tkv = b.op(V, lambda e, s=ksl:
                       e.tensor_copy(krstage[:, s * 512:(s + 1) * 512],
                                     PM[0][0:32, :]))
            sPM[0].release(V, tkv)
            b.wt(G, V, tkv)
            tkd = b.dma_slot(G, "krs%d" % ksl,
                             [(krTraw[32 * (tc // 2):32 * (tc // 2) + 32,
                                      (tc % 2) * 512:(tc % 2 + 1) * 512],
                               krstage[:, ksl * 512:(ksl + 1) * 512])])
            skrst[ksl].release(G, tkd)
            tok_kr_dma.append(tkd)

    wq = [("wdkv", wdkv_d, wDKVb, D, DL), ("wdq", wdq_d, wDQb, D, DL),
          ("wkr", wkr_d, wKRb, D, 32), ("wuk", wuk_d, wUKb, DL, 512),
          ("wuv", wuv_d, wUVb, DL, 512), ("wuq", wuq_d, wUQb, DL, 512),
          ("wqr", wqr_d, wQRb, DL, 256)]
    wissue = {0: 0, 1: 1, 2: 2, 3: 3, 8: 4, 9: 5, 10: 6}
    wcast = {5: 0, 6: 1, 7: 2}
    tok_cs = [None]
    sched = {}
    for i in range(39):
        for (kind, tcx, part) in sched.pop(i, []):
            if kind == "ev":
                a0_proj_ev(tcx, part)
            else:
                a0_proj_mm(tcx, part)
        if i < 32:
            a0_stage1(i)
        if 1 <= i < 33:
            a0_stage2(i - 1)
        if 2 <= i < 34:
            a0_stage3(i - 2)
        if 3 <= i < 35:
            a0_stage4(i - 3)
            if (i - 3) % 4 == 3:
                tcx = (i - 3) // 4
                sched.setdefault(i + 1, []).append(("mm", tcx, 0))
                sched.setdefault(i + 2, []).extend(
                    [("ev", tcx, 0), ("mm", tcx, 1)])
                sched.setdefault(i + 3, []).extend(
                    [("ev", tcx, 1), ("mm", tcx, 2)])
                sched.setdefault(i + 4, []).append(("ev", tcx, 2))
        if i in wcast:
            weight_cast(wcast[i])
        if i in wissue:
            weight_issue(wissue[i])
        if i == 10:
            tok_cs[0] = b.dma_slot(SY, "cs", [(coskA[:], cosk_d),
                                              (sinkA[:], sink_d),
                                              (hoffs[:], hoff_d)])
    for widx in (3, 4, 5, 6):
        weight_cast(widx)

    b.wtd(T, tok_kr_dma[-2])
    b.wtd(T, tok_kr_dma[-1])
    sPS[0].acquire(T)
    for hseg in range(2):
        tsw = b.op(T, lambda e, h=hseg:
                   e.matmul(PS[0][:, h * 512:(h + 1) * 512], p128b[:],
                            krTraw[:, h * 512:(h + 1) * 512],
                            start=True, stop=True))
    b.wt(V, T, tsw)
    b.wtd(V, tok_cs[0])
    b.op(V, lambda e: e.tensor_tensor(xst[:, 0:1024], krTraw[:], coskA[:], ALU.mult))
    b.op(V, lambda e: e.tensor_tensor(junkA[:], PS[0][:], sinkA[:], ALU.mult))
    b.sw(V)
    tkrope = b.op(V, lambda e: e.tensor_tensor(krTb[:], xst[:, 0:1024], junkA[:],
                                               ALU.add))
    sPS[0].release(V, tkrope)

    emit_taps("a0")
    b.join()
    es_a0.close()
    K_all = sba("K_all", [128, HL * S])
    V_grp = sba("V_grp", [128, NKC * HL * 65])

    def vsl(kc, h):
        base = (kc * HL + h) * 65
        return V_grp[:, base: base + 65]

    ones_ap = V_grp[:].rearrange("p (g j) -> p g j", j=65)[:, :, 64:65]
    tvg = b.op(V, lambda e: e.memset(ones_ap, 1.0))
    b.wt(T, V, wtok["wuk"])
    b.wt(T, V, wtok["wuv"])
    b.wt(T, V, tvg)
    rotp = [PM[0], PM[1], PA[0], PA[1]]
    rots = [sPM[0], sPM[1], sPA[0], sPA[1]]
    ridx = [0]

    def nextp():
        i = ridx[0] % 4
        ridx[0] += 1
        return rotp[i], rots[i]

    for run in range(8):
        for mc in range(4):
            p, psl_ = nextp()
            psl_.acquire(T)
            for k in range(2):
                tke = b.op(T, lambda e, mc=mc, k=k, run=run, p=p:
                           e.matmul(p[:],
                                    wUKb[:, k * 512 + mc * 128:
                                         k * 512 + (mc + 1) * 128],
                                    c_kvT[:, k * S + run * 512:
                                          k * S + (run + 1) * 512],
                                    start=(k == 0), stop=(k == 1)))
            b.wt(V, T, tke)
            he, ho = 2 * mc, 2 * mc + 1
            tev = b.op(V, lambda e, h=he, run=run, p=p:
                       e.tensor_copy(K_all[0:64, h * S + run * 512:
                                           h * S + (run + 1) * 512],
                                     p[0:64, :]))
            osl = (run * 4 + mc) % 2
            qeng = G if osl else SY
            b.wt(SG, T, tke)
            soddst[osl].acquire(SG)
            tex = b.op(SG, lambda e, p=p, o=osl:
                       e.copy(oddst[64:128, o * 512:(o + 1) * 512],
                              p[64:128, :]))
            psl_.release(V, tev)
            psl_.release(SG, tex)
            b.wt(qeng, SG, tex)
            tkd = b.dma_slot(qeng, "odd%d" % osl,
                             [(K_all[0:64, ho * S + run * 512:
                                     ho * S + (run + 1) * 512],
                               oddst[64:128, osl * 512:(osl + 1) * 512])])
            soddst[osl].release(qeng, tkd)
        for j in range(4):
            kc = run * 4 + j
            p, psl_ = nextp()
            psl_.acquire(T)
            for k in range(2):
                tvm = b.op(T, lambda e, kc=kc, k=k, p=p:
                           e.matmul(p[:], c_kvT[:, k * S + kc * 128:
                                                k * S + (kc + 1) * 128],
                                    wUVb[:, k * 512:(k + 1) * 512],
                                    start=(k == 0), stop=(k == 1)))
            ev_eng = V if j % 2 == 0 else SG
            b.wt(ev_eng, T, tvm)
            dstv = V_grp[:, kc * 520:(kc + 1) * 520].rearrange(
                "p (h j) -> p h j", j=65)[:, :, 0:64]
            srcv = p[:].rearrange("p (h j) -> p h j", j=64)
            if ev_eng is V:
                tvev = b.op(V, lambda e, d=dstv, s=srcv: e.tensor_copy(d, s))
            else:
                tvev = b.op(SG, lambda e, d=dstv, s=srcv: e.copy(d, s))
            psl_.release(ev_eng, tvev)
    b.wt(G, V, tkrope)
    b.dma_group(G, [(K_all[64:96, h * S + ch * 1024: h * S + (ch + 1) * 1024],
                     krTb[32 * ch:32 * ch + 32, :])
                    for h in range(HL) for ch in range(4)])
    b.join()
    emit_taps("expand")

    tok_cc_last = 0
    exp_hist = []
    av_hist = []
    pend_av = []
    pend_eps = []
    ep_ssl = [0]
    tcollA = [0]

    def av_hist_tok(k):
        return av_hist[k]

    sdiv = [Slot(b), Slot(b)]

    def emit_qexp(j2):
        qof = j2 * 512
        qsl = 0
        qb = (j2 % 4) * HL * 512
        stabs[qsl].acquire(SY)
        ttab = b.dma_slot(SY, "tb0", [
            (cosqs[:], cosq_d[:, qof:qof + 512]),
            (sinqs[:], sinq_d[:, qof:qof + 512])])
        sQTl[j2 % 4].acquire(V)
        tq_v = 0
        tg_toks = []
        for mc in range(4):
            psl = mc % 2
            sPM[psl].acquire(T)
            for k in range(2):
                tqm = b.op(T, lambda e, mc=mc, k=k, qof=qof, p=PM[psl]:
                           e.matmul(p[:],
                                    wUQb[:, k * 512 + mc * 128:
                                         k * 512 + (mc + 1) * 128],
                                    c_qT[:, k * S + qof: k * S + qof + 512],
                                    start=(k == 0), stop=(k == 1)))
            b.wt(V, T, tqm)
            he, ho = 2 * mc, 2 * mc + 1
            tq_v = b.op(V, lambda e, h=he, p=PM[psl], qb=qb:
                        e.tensor_copy(QT[0:64, qb + h * 512:
                                          qb + (h + 1) * 512],
                                      p[0:64, :]))
            osl = mc % 2
            soddst[osl].acquire(V)
            tqo = b.op(V, lambda e, p=PM[psl], o=osl:
                       e.tensor_copy(oddst[64:128, o * 512:(o + 1) * 512],
                                     p[64:128, :]))
            sPM[psl].release(V, tqo)
            b.wt(G, V, tqo)
            tq_g_qc = b.dma_slot(G, "odd%d" % osl,
                                 [(QT[0:64, qb + ho * 512:
                                      qb + (ho + 1) * 512],
                                   oddst[64:128, osl * 512:(osl + 1) * 512])])
            tg_toks.append(tq_g_qc)
            soddst[osl].release(G, tq_g_qc)
        b.wtd(V, ttab)
        for rc in range(2):
            psl = rc % 2
            sPM[psl].acquire(T)
            for k in range(2):
                tqr = b.op(T, lambda e, rc=rc, k=k, qof=qof, p=PM[psl]:
                           e.matmul(p[:],
                                    wQRb[:, k * 256 + rc * 128:
                                         k * 256 + (rc + 1) * 128],
                                    c_qT[:, k * S + qof: k * S + qof + 512],
                                    start=(k == 0), stop=(k == 1)))
            b.wt(V, T, tqr)
            sqr[rc].acquire(V)
            traw = b.op(V, lambda e, rc=rc, p=PM[psl]:
                        e.tensor_copy(qrraw[:, rc * 512:(rc + 1) * 512], p[:]))
            sPM[psl].release(V, traw)
            sPM[psl].acquire(T)
            b.wt(T, V, traw)
            tswp = b.op(T, lambda e, rc=rc, p=PM[psl]:
                        e.matmul(p[:], p128b[:],
                                 qrraw[:, rc * 512:(rc + 1) * 512],
                                 start=True, stop=True))
            b.wt(V, T, tswp)
            b.sw(V)
            b.op(V, lambda e, rc=rc:
                 e.tensor_tensor(tmpq[:], qrraw[:, rc * 512:(rc + 1) * 512],
                                 cosqs[:], ALU.mult))
            tr2 = b.op(V, lambda e, rc=rc, p=PM[psl]:
                       e.tensor_tensor(qrrope[:, rc * 512:(rc + 1) * 512],
                                       p[:], sinqs[:], ALU.mult))
            sPM[psl].release(V, tr2)
            b.sw(V)
            trp = b.op(V, lambda e, rc=rc:
                       e.tensor_tensor(qrrope[:, rc * 512:(rc + 1) * 512],
                                       tmpq[:],
                                       qrrope[:, rc * 512:(rc + 1) * 512],
                                       ALU.add))
            stabs[0].release(V, trp)
            b.wt(G, V, trp)
            tq_g = b.dma_slot(G, "qr%d" % rc, [
                (QT[64:96, qb + (4 * rc + h4) * 512:
                    qb + (4 * rc + h4 + 1) * 512],
                 qrrope[32 * h4:32 * h4 + 32, rc * 512:(rc + 1) * 512])
                for h4 in range(4)])
            tg_toks.append(tq_g)
            sqr[rc].release(G, tq_g)
        latest = {}
        for t in tg_toks:
            latest[t[1]] = t
        return (trp, list(latest.values()))

    qexp_tok = {0: emit_qexp(0), 1: emit_qexp(1)}

    def emit_av(d):
        b.wt(T, V, d["tclip"])
        if d["st"]:
            while any(ep["acc"] == d["acc"] for ep in pend_eps):
                step_eps(force=True)
            sdiv[d["acc"]].acquire(T)
        b.op(T, lambda e, kc=d["kcA"], h=d["h"], s2=d["sA"], w_=d["wA"],
             sl_=d["pslot"], st=d["st"], p=PA[d["acc"]]:
             e.matmul(p[0:65, s2:512], vsl(kc, h),
                      pbuf[:, sl_ * 1024: sl_ * 1024 + w_],
                      start=st, stop=False))
        tav = b.op(T, lambda e, kc=d["kcB"], h=d["h"], s2=d["sB"],
                   w_=d["wB"], sl_=d["pslot"], last=d["last"],
                   p=PA[d["acc"]]:
                   e.matmul(p[0:65, s2:512], vsl(kc, h),
                            pbuf[:, sl_ * 1024 + 512:
                                 sl_ * 1024 + 512 + w_],
                            start=False, stop=last))
        spb[d["pslot"]].release(T, tav)
        av_hist.append(tav)
        if d["last"]:
            h_, acc_, j_ = d["h"], d["acc"], d["j"]
            dsl = j_ % 2
            b.wt(SG, T, tav)
            sden[dsl].acquire(SG)
            tl1 = b.op(SG, lambda e, dd=dsl, p=PA[acc_]:
                       e.activation(den[64:65, dd * 512:(dd + 1) * 512],
                                    p[64:65, 0:512], AF.Ln))
            b.sw(SG)
            tl2 = b.op(SG, lambda e, dd=dsl:
                       e.activation(den[64:65, dd * 512:(dd + 1) * 512],
                                    den[64:65, dd * 512:(dd + 1) * 512],
                                    AF.Exp, scale=-1.0))
            b.wt(V, SG, b.cnt[SG])
            b.op(V, lambda e, dd=dsl:
                 e.tensor_copy(denb[64:65, dd * 1024:dd * 1024 + 512],
                               den[64:65, dd * 512:(dd + 1) * 512]))
            b.sw(V)
            tden = b.op(V, lambda e, dd=dsl: e.tensor_tensor(
                denb[64:65, dd * 1024 + 512:dd * 1024 + 1024],
                den[64:65, dd * 512:(dd + 1) * 512],
                denb[64:65, dd * 1024:dd * 1024 + 512], ALU.subtract))
            pend_eps.append({"h": h_, "j": j_, "acc": acc_, "dsl": dsl,
                             "pmr": j_ % 2, "tden": tden, "stage": 0,
                             "born": len(exp_hist)})

    def step_eps(force=False):
        if not pend_eps:
            return
        ep = pend_eps[0]
        if not force and len(exp_hist) - ep["born"] < 10:
            return
        if ep["stage"] == 0:
            b.wt(T, V, ep["tden"])
            sPM[ep["pmr"]].acquire(T)
            b.op(T, lambda e, d=ep["dsl"], p=PM[ep["pmr"]]:
                 e.matmul(p[0:64, :], ones64b[64:65, 0:64],
                          denb[64:65, d * 1024:d * 1024 + 512],
                          start=True, stop=False))
            ep["trep"] = b.op(T, lambda e, d=ep["dsl"], p=PM[ep["pmr"]]:
                              e.matmul(p[0:64, :], ones64b[64:65, 0:64],
                                       denb[64:65, d * 1024 + 512:
                                            d * 1024 + 1024],
                                       start=False, stop=True))
            sden[ep["dsl"]].release(T, ep["trep"])
            ep["stage"] = 1
            return
        b.wt(V, T, ep["trep"])
        srepsb[ep["pmr"]].acquire(V)
        trsb = b.op(V, lambda e, d=ep["pmr"], p=PM[ep["pmr"]]:
                    e.tensor_copy(repsb[:, d * 512:(d + 1) * 512],
                                  p[0:64, :]))
        sPM[ep["pmr"]].release(V, trsb)
        ssl = ep_ssl[0]
        ep_ssl[0] = (ep_ssl[0] + 1) % 3
        b.sw(V)
        sstage[ssl].acquire(V)
        tdiv = b.op(V, lambda e, d=ep["pmr"], sl_=ssl, p=PA[ep["acc"]]:
                    e.tensor_tensor(stage[:, sl_ * 512:(sl_ + 1) * 512],
                                    p[0:64, 0:512],
                                    repsb[:, d * 512:(d + 1) * 512],
                                    ALU.mult))
        srepsb[ep["pmr"]].release(V, tdiv)
        sdiv[ep["acc"]].release(V, tdiv)
        b.wt(G, V, tdiv)
        ccbuf = cc_inA if ep["j"] < 4 else cc_inB
        jj = ep["j"] % 4
        tcc = b.dma_slot(
            G, "st%d" % ssl,
            [(ccbuf[ep["h"] // 2, (ep["h"] % 2) * 64:
                    (ep["h"] % 2) * 64 + 64,
                    jj * 512:(jj + 1) * 512],
              stage[:, ssl * 512:(ssl + 1) * 512])])
        sstage[ssl].release(G, tcc)
        pend_eps.pop(0)

    def emit_wave(st):
        j = st["j"]
        qb = (j % 4) * HL * 512
        h, w, nw = st["h"], st["w"], st["nw"]
        acc = j % 2
        kcA, kcB = 2 * w, 2 * w + 1
        sA = 128 * max(0, kcA - 4 * j)
        sB = 128 * max(0, kcB - 4 * j)
        wA, wB = 512 - sA, 512 - sB
        gw = len(exp_hist)
        slot = gw % 2
        pslot = gw % 4
        sPS[slot].acquire(T)
        if h == 0 and w == 0:
            (tv, tgs) = qexp_tok[j]
            b.wt(T, V, tv)
            for tg in tgs:
                b.wtd(T, tg)
        b.op(T, lambda e, h=h, kc=kcA, s=sA, w_=wA, p=PS[slot], qb=qb:
             e.matmul(p[:, 0:w_],
                      K_all[0:96, h * S + kc * 128: h * S + (kc + 1) * 128],
                      QT[0:96, qb + h * 512 + s: qb + (h + 1) * 512],
                      start=True, stop=True))
        tsc = b.op(T, lambda e, h=h, kc=kcB, s=sB, w_=wB, p=PS[slot], qb=qb:
                   e.matmul(p[:, 512:512 + w_],
                            K_all[0:96, h * S + kc * 128:
                                  h * S + (kc + 1) * 128],
                            QT[0:96, qb + h * 512 + s: qb + (h + 1) * 512],
                            start=True, stop=True))
        sQTl[j % 4].release(T, tsc)
        while len(pend_av) > 2:
            emit_av(pend_av.pop(0))
        step_eps()
        b.wt(SG, T, tsc)
        if gw >= 4:
            b.wt(SG, T, av_hist_tok(gw - 4))
        spb[pslot].acquire(SG)
        texp = b.op(SG, lambda e, slot=pslot, wB=wB, p=PS[slot]:
                    e.activation(pbuf[:, slot * 1024: slot * 1024 + 512 + wB],
                                 p[:, 0:512 + wB], AF.Exp, scale=SCALE))
        sPS[slot].release(SG, texp)
        exp_hist.append(texp)
        b.wt(V, SG, texp)
        tclip = b.op(V, lambda e, slot=pslot, wB=wB:
                     e.tensor_scalar(
                         pbuf[:, slot * 1024: slot * 1024 + 512 + wB],
                         pbuf[:, slot * 1024: slot * 1024 + 512 + wB],
                         CLIP_HI, CLIP_LO, ALU.min, ALU.max))
        if kcA >= 4 * j or kcB >= 4 * j:
            b.sw(V)
        if kcA >= 4 * j:
            tclip = b.op(V, lambda e, slot=pslot:
                         e.tensor_tensor(
                             pbuf[:, slot * 1024: slot * 1024 + 128],
                             pbuf[:, slot * 1024: slot * 1024 + 128],
                             triub[:], ALU.mult))
        if kcB >= 4 * j:
            tclip = b.op(V, lambda e, slot=pslot:
                         e.tensor_tensor(
                             pbuf[:, slot * 1024 + 512:
                                  slot * 1024 + 640],
                             pbuf[:, slot * 1024 + 512:
                                  slot * 1024 + 640],
                             triub[:], ALU.mult))
        pend_av.append(dict(kcA=kcA, kcB=kcB, sA=sA, sB=sB, wA=wA, wB=wB,
                            pslot=pslot, tclip=tclip, h=h, j=j, acc=acc,
                            st=(w == 0), last=(w == nw - 1)))
        st["w"] += 1
        if st["w"] == nw:
            st["w"] = 0
            st["h"] += 1
            if st["h"] == 1 and st["j"] + 2 < NQC:
                qexp_tok[st["j"] + 2] = emit_qexp(st["j"] + 2)
        return st["h"] < HL

    for J0 in range(0, NQC, 2):
        streams = [dict(j=J0, h=0, w=0, nw=2 * J0 + 2),
                   dict(j=J0 + 1, h=0, w=0, nw=2 * J0 + 4)]
        while streams:
            for st in list(streams):
                if not emit_wave(st):
                    streams.remove(st)
        while pend_av:
            emit_av(pend_av.pop(0))
            step_eps(force=True)
        while pend_eps:
            step_eps(force=True)
        if J0 == 2:
            for kslot in range(3):
                ek = (G, "st%d" % kslot)
                if ek in b.ssem:
                    b.wtd(G, ("d", ek, b.scnt[ek]))
            tcollA[0] = b.op(G, lambda e: e.collective_compute(
                "AllGather", ALU.bypass,
                replica_groups=[[0, 1], [2, 3], [4, 5], [6, 7]],
                ins=[cc_inA.opt()], outs=[cc_outA_h.ap().opt()]), 1)

    emit_taps("attn")
    b.join()
    for kslot in range(3):
        ek = (G, "st%d" % kslot)
        if ek in b.ssem:
            b.wtd(G, ("d", ek, b.scnt[ek]))
    tcoll = b.op(G, lambda e: e.collective_compute(
        "AllGather", ALU.bypass,
        replica_groups=[[0, 1], [2, 3], [4, 5], [6, 7]],
        ins=[cc_inB.opt()], outs=[cc_outB_h.ap().opt()]), 1)

    es_a.close()
    es_b = contextlib.ExitStack()
    sbb = mk_sb(es_b)
    attnT = sbb("attnT", [128, 8 * 2048])
    woB = sbb("woB", [128, 8 * 1024])
    wffT = sbb("wffT", [128, 8 * 1024])
    wpT = sbb("wpT", [128, 8 * 256])
    wdT = sbb("wdT", [128, 8 * 15])
    wuT = sbb("wuT", [15, 1024])
    wdcat = sbb("wdcat", [15, 1024])
    simnB = sbb("simnB", [128, 2 * 4])
    simf = sbb("simf", [128, 2 * 4], F32)
    bpB = sbb("bpB", [128, 2], F32)
    yb = sbb("yb", [128, 8 * 1024], F32)
    h2T = sbb("h2T", [128, 8 * 1024])
    h2tok = sbb("h2tok", [128, 2 * 1024])
    junkB = sbb("junkB", [128, 1024], F32)
    projT = sbb("projT", [128, 2 * 512])
    projsq = sbb("projsq", [128, 512])
    retT = sbb("retT", [4, 1024])
    down15 = sbb("down15", [15, 1024])
    scaled15 = sbb("scaled15", [15, 1024])
    zsb = sbb("zsb", [128, 2 * 8 * 512])
    ffnsb = sbb("ffnsb", [128, 2 * 8 * 512])
    xresb = sbb("xresb", [128, 2 * 1024], F32)
    outsb = sbb("outsb", [128, 2 * 1024], F32)
    wstB = sbb("wstB", [128, 2 * 1024], F32)
    wcast = sbb("wcast", [128, 1024])
    lgsb = sbb("lgsb", [128, 4], F32)
    esb = sbb("esb", [128, 4], F32)
    mskb = sbb("mskb", [128, 4], F32)
    retb = sbb("retb", [128, 4])
    invT = sbb("invT", [128, 4], F32)
    invs = sbb("invs", [1, 512], F32)
    sq1 = sbb("sq1", [1, 2 * 512], F32)

    swB = [Slot(b), Slot(b)]
    sjB = Slot(b)
    sxr = [Slot(b), Slot(b)]
    sh2tok = [Slot(b), Slot(b)]
    souts = [Slot(b), Slot(b)]
    szsb = [Slot(b), Slot(b)]
    sffn = Slot(b)
    sffnb = [Slot(b), Slot(b)]
    sproj = Slot(b)
    sretT = Slot(b)
    sy = [Slot(b) for _ in range(8)]
    sstatB = [Slot(b), Slot(b)]
    sgate = Slot(b)

    def loadcast(dst, dstoff, cols, srcap, idx=[0]):
        sl = idx[0] % 2
        idx[0] += 1
        swB[sl].acquire(SY)
        td = b.dma_slot(SY, "wB%d" % sl,
                        [(wstB[:, sl * 1024: sl * 1024 + cols], srcap)])
        b.wtd(V, td)
        tc_ = b.op(V, lambda e, d=dst, o=dstoff, c=cols, s=sl:
                   e.tensor_copy(d[:, o:o + c],
                                 wstB[:, s * 1024: s * 1024 + c]))
        swB[sl].release(V, tc_)
        return tc_

    for k in range(8):
        loadcast(woB, k * 1024, 1024, wo_d[k * 128:(k + 1) * 128, :])

    def transposed_weight(src_d, nrow_chunks, dstT, cols_per_chunk):
        for r in range(nrow_chunks):
            sjB.acquire(V)
            tcast = loadcast(wcast, 0, 1024, src_d[r * 128:(r + 1) * 128, :])
            psl = r % 2
            b.wt(T, V, tcast)
            sPS[psl].acquire(T)
            for k in range(8):
                ttr = b.op(T, lambda e, k=k, p=PSB[psl]:
                           e.transpose(p[:, k * 128:(k + 1) * 128],
                                       wcast[:, k * 128:(k + 1) * 128],
                                       identB[:]))
            sjB.release(T, ttr)
            b.wt(V, T, ttr)
            dstw = dstT[:].rearrange("p (k c) -> p k c",
                                     k=8)[:, :, r * 128:(r + 1) * 128]
            srcw = PSB[psl][:, 0:1024].rearrange("p (k c) -> p k c", k=8)
            tevw = b.op(V, lambda e, d=dstw, s=srcw: e.tensor_copy(d, s))
            sPS[psl].release(V, tevw)
        return tevw

    transposed_weight(wff_d, 8, wffT, 1024)
    for pc in range(2):
        sjB.acquire(V)
        tcast = loadcast(wcast, 0, 1024, wp_d[pc * 128:(pc + 1) * 128, :])
        psl = pc % 2
        b.wt(T, V, tcast)
        sPS[psl].acquire(T)
        for k in range(8):
            ttr = b.op(T, lambda e, k=k, p=PSB[psl]:
                       e.transpose(p[:, k * 128:(k + 1) * 128],
                                   wcast[:, k * 128:(k + 1) * 128],
                                   identB[:]))
        sjB.release(T, ttr)
        b.wt(V, T, ttr)
        dstw = wpT[:].rearrange("p (k c) -> p k c",
                                k=8)[:, :, pc * 128:(pc + 1) * 128]
        srcw = PSB[psl][:, 0:1024].rearrange("p (k c) -> p k c", k=8)
        tevw = b.op(V, lambda e, d=dstw, s=srcw: e.tensor_copy(d, s))
        sPS[psl].release(V, tevw)
    swB[0].acquire(SY)
    td = b.dma(SY, wstB[0:15, 0:1024], wd_d)
    b.wt(V, SY, td)
    tcast = b.op(V, lambda e: e.tensor_copy(wdcat[:], wstB[0:15, 0:1024]))
    swB[0].release(V, tcast)
    b.wt(T, V, tcast)
    sPS[0].acquire(T)
    for k in range(8):
        ttr = b.op(T, lambda e, k=k:
                   e.transpose(PSB[0][:, k * 128:k * 128 + 15],
                               wdcat[:, k * 128:(k + 1) * 128],
                               identB[0:15, 0:15]))
    b.wt(V, T, ttr)
    for k in range(8):
        tevw = b.op(V, lambda e, k=k:
                    e.tensor_copy(wdT[:, k * 15:(k + 1) * 15],
                                  PSB[0][:, k * 128:k * 128 + 15]))
    sPS[0].release(V, tevw)
    for k in range(8):
        sl = k % 2
        swB[sl].acquire(SY)
        td = b.dma(SY, wstB[:, sl * 1024: sl * 1024 + 15],
                   wu_d[k * 128:(k + 1) * 128, :])
        b.wt(V, SY, td)
        sjB.acquire(V)
        tcast = b.op(V, lambda e, s=sl: e.tensor_copy(
            wcast[:, 0:15], wstB[:, s * 1024: s * 1024 + 15]))
        swB[sl].release(V, tcast)
        b.wt(T, V, tcast)
        sPS[1].acquire(T)
        ttr = b.op(T, lambda e: e.transpose(PSB[1][0:15, 0:128],
                                            wcast[:, 0:15], identB[:]))
        sjB.release(T, ttr)
        b.wt(V, T, ttr)
        tevw = b.op(V, lambda e, k=k:
                    e.tensor_copy(wuT[0:15, k * 128:(k + 1) * 128],
                                  PSB[1][0:15, 0:128]))
        sPS[1].release(V, tevw)
    b.dma(SY, simf[:, 0:4], sim_d[0:128, :])
    tsd_ = b.dma(SY, simf[:, 4:8], sim_d[128:256, :])
    b.wt(SG, SY, tsd_)
    sjB.acquire(SG)
    tsq = b.op(SG, lambda e: e.activation(wcast[:, 0:8], simf[:], AF.Square))
    b.wt(T, SG, tsq)
    sPS[1].acquire(T)
    for k in range(2):
        tsm = b.op(T, lambda e, k=k:
                   e.matmul(PS[1][0:1, 0:4], onesb[:],
                            wcast[:, k * 4:(k + 1) * 4],
                            start=(k == 0), stop=(k == 1)))
    sjB.release(T, tsm)
    b.wt(SG, T, tsm)
    tsq2 = b.op(SG, lambda e: e.activation(sq1[:, 0:4], PS[1][0:1, 0:4],
                                           AF.Sqrt))
    sPS[1].release(SG, tsq2)
    b.wt(V, SG, tsq2)
    tvr = b.op(V, lambda e: e.reciprocal(sq1[:, 4:8], sq1[:, 0:4]))
    b.sw(V)
    b.wt(T, V, tvr)
    sPS[1].acquire(T)
    trep = b.op(T, lambda e: e.matmul(PS[1][0:128, 0:4], ones1f[:],
                                      sq1[0:1, 4:8], start=True, stop=True))
    b.wt(V, T, trep)
    for k in range(2):
        tsimn = b.op(V, lambda e, k=k:
                     e.tensor_tensor(simnB[:, k * 4:(k + 1) * 4],
                                     simf[:, k * 4:(k + 1) * 4],
                                     PS[1][:, 0:4], ALU.mult))
    sPS[1].release(V, tsimn)
    b.dma(SY, bpB[:, 0:1], bp_d.rearrange("(p c) -> p c", c=1)[0:128])
    b.dma(SY, bpB[:, 1:2], bp_d.rearrange("(p c) -> p c", c=1)[128:256])

    roff = nc.alloc_register(mybir.EngineType.Pool, "roff")
    b.wt(G, G, tcollA[0])
    b.wt(G, G, tcoll)
    rowe = 16 * 2048
    dsemG = b.dsem[G]
    for k16 in range(16):
        bufh = cc_outA_h if k16 < 8 else cc_outB_h
        sidx, lc = (k16 % 8) // 4, (k16 % 8) % 4
        b.on(G, lambda e, k16=k16: e.reg_load(roff, hoffs[0:1, k16:k16 + 1]))
        srcap = bufh.ap()[sidx, lc]
        b.dcnt[G] += 16
        b.q[G].append((
            lambda e, srcap=srcap, ds=dsemG:
            e.dma_start(out=bass.AP(ccrb_h, roff, [[rowe, 128], [1, 2048]]),
                        in_=srcap).then_inc(ds, 16), 0, None))
    b.q[G].append((lambda e, ds=dsemG, t=b.dcnt[G]: e.wait_ge(ds, t),
                   0, None))
    b.cnt[G] += 1
    b.q[G].append((None, 1, "pub"))
    b.dma_group(G, [(attnT[:, kq * 4096:(kq + 1) * 4096],
                     ccrb_h.ap()[:, kq * 4096:(kq + 1) * 4096])
                    for kq in range(4)])
    b.join()

    sPMb = sPM
    wo_tok = {}
    h2_tok = {}
    ret_tok = {}
    s15_tok = {}
    ffn_tok = {}

    def wo_gen(sh, q2):
        c0 = sh * 1024 + q2 * 512
        zq = zsb[:, q2 * 4096:(q2 + 1) * 4096]
        szsb[q2].acquire(SG)
        for oc in range(8):
            psl = oc % 2
            sPMb[psl].acquire(T)
            for k in range(8):
                tz = b.op(T, lambda e, k=k, oc=oc, c0=c0, p=PM[psl]:
                          e.matmul(p[:],
                                   woB[:, k * 1024 + oc * 128:
                                       k * 1024 + (oc + 1) * 128],
                                   attnT[:, k * 2048 + c0:
                                         k * 2048 + c0 + 512],
                                   start=(k == 0), stop=(k == 7)))
            b.wt(SG, T, tz)
            tze = b.op(SG, lambda e, oc=oc, zq=zq, p=PM[psl]:
                       e.copy(zq[:, oc * 512:(oc + 1) * 512], p[:]))
            sPMb[psl].release(SG, tze)
            wo_tok[(sh, q2)] = tze
            if oc % 2 == 1:
                yield

    def bh1(sh, q2, ti):
        t8 = q2 * 4 + ti
        xsl = t8 % 2
        ss = xsl * 16
        zq = zsb[:, q2 * 4096:(q2 + 1) * 4096]
        sxr[xsl].acquire(SY)
        txr = b.dma_slot(SY, "xr%d" % xsl,
                         [(xresb[:, xsl * 1024:(xsl + 1) * 1024],
                           xres_d[(sh * 8 + t8) * 128:
                                  (sh * 8 + t8 + 1) * 128, :])])
        psl = ti % 2
        b.wt(T, SG, wo_tok[(sh, q2)])
        sPS[psl].acquire(T)
        for oc in range(8):
            ttz = b.op(T, lambda e, oc=oc, ti=ti, zq=zq, p=PSB[psl]:
                       e.transpose(p[:, oc * 128:(oc + 1) * 128],
                                   zq[:, oc * 512 + ti * 128:
                                      oc * 512 + (ti + 1) * 128],
                                   identB[:]))
        if ti == 3:
            szsb[q2].release(T, ttz)
        b.wt(V, T, ttz)
        b.wtd(V, txr)
        sy[t8].acquire(V)
        yt = yb[:, t8 * 1024:(t8 + 1) * 1024]
        ty = b.op(V, lambda e, yt=yt, xsl=xsl, p=PSB[psl]:
                  e.tensor_tensor(yt,
                                  xresb[:, xsl * 1024:(xsl + 1) * 1024],
                                  p[:, 0:1024], ALU.add))
        sPS[psl].release(V, ty)
        sxr[xsl].release(V, ty)
        sstatB[xsl].acquire(V)
        b.sw(V)
        b.op(V, lambda e, yt=yt, o=sc(ss + 0):
             e.reduce_sum(o, yt, axis=AX))
        tms = b.op(V, lambda e, o=sc(ss + 1): e.memset(o, 0.0))
        b.wt(SG, V, tms)
        sjB.acquire(SG)
        tsq = b.op(SG, lambda e, yt=yt, o=sc(ss + 1):
                   e.activation(junkB[:], yt, AF.Square, accum_out=o))
        return {"tsq": tsq, "psl": psl, "xsl": xsl, "ss": ss,
                "yt": yt, "t8": t8}

    def bh2(st):
        ss, xsl, psl, yt = st["ss"], st["xsl"], st["psl"], st["yt"]
        b.wt(V, SG, st["tsq"])
        b.op(V, lambda e, a=sc(ss + 0), o=sc(ss + 2):
             e.tensor_single_scalar(o, a, -1.0 / 1024.0, ALU.mult))
        b.op(V, lambda e, a=sc(ss + 1), o=sc(ss + 3):
             e.tensor_single_scalar(o, a, 1.0 / 1024.0, ALU.mult))
        b.sw(V)
        b.op(V, lambda e, a=sc(ss + 2), o=sc(ss + 4):
             e.tensor_tensor(o, a, a, ALU.mult))
        b.sw(V)
        b.op(V, lambda e, a=sc(ss + 3), c=sc(ss + 4), o=sc(ss + 5):
             e.tensor_tensor(o, a, c, ALU.subtract))
        b.sw(V)
        tv = b.op(V, lambda e, a=sc(ss + 5), o=sc(ss + 5):
                  e.tensor_single_scalar(o, a, LN_EPS, ALU.add))
        b.wt(SG, V, tv)
        b.op(SG, lambda e, a=sc(ss + 5), o=sc(ss + 6):
             e.activation(o, a, AF.Ln))
        b.sw(SG)
        b.op(SG, lambda e, a=sc(ss + 6), o=sc(ss + 7):
             e.activation(o, a, AF.Exp, scale=-0.5))
        b.sw(SG)
        b.op(SG, lambda e, m=sc(ss + 2), r=sc(ss + 7), o=sc(ss + 8):
             e.activation(o, m, AF.Identity, scale=r))
        b.sw(SG)
        sh2tok[xsl].acquire(SG)
        th2 = b.op(SG, lambda e, yt=yt, hsl=xsl, r=sc(ss + 7),
                   bi=sc(ss + 8):
                   e.activation(h2tok[:, hsl * 1024:(hsl + 1) * 1024],
                                yt, AF.Identity, bias=bi, scale=r))
        sjB.release(SG, th2)
        sstatB[xsl].release(SG, th2)
        sy[st["t8"]].release(SG, th2)
        b.wt(T, SG, th2)
        sPS[psl].acquire(T)
        for k in range(8):
            tth = b.op(T, lambda e, k=k, hsl=xsl, p=PSB[psl]:
                       e.transpose(p[:, k * 128:(k + 1) * 128],
                                   h2tok[:, hsl * 1024 + k * 128:
                                         hsl * 1024 + (k + 1) * 128],
                                   identB[:]))
        sh2tok[xsl].release(T, tth)
        b.wt(V, T, tth)
        dsth = h2T[:].rearrange("p (k t) -> p k t", k=8)[
            :, :, st["t8"] * 128:(st["t8"] + 1) * 128]
        srch = PSB[psl][:, 0:1024].rearrange("p (k t) -> p k t", k=8)
        th2e = b.op(V, lambda e, d=dsth, s2=srch: e.tensor_copy(d, s2))
        sPS[psl].release(V, th2e)
        return th2e

    def tiles_gen(sh, q2):
        pend = []
        for ti in range(5):
            if ti < 4:
                pend.append(bh1(sh, q2, ti))
            if ti >= 1:
                h2_tok[(sh, q2)] = bh2(pend.pop(0))
            yield

    def router_block(sh, q2):
        b.wt(T, V, h2_tok[(sh, q2)])
        sproj.acquire(SG)
        for pc in range(2):
            psl = pc % 2
            sPMb[psl].acquire(T)
            for k in range(8):
                tp = b.op(T, lambda e, k=k, pc=pc, q2=q2, p=PM[psl]:
                          e.matmul(p[:],
                                   wpT[:, k * 256 + pc * 128:
                                       k * 256 + (pc + 1) * 128],
                                   h2T[:, k * 1024 + q2 * 512:
                                       k * 1024 + (q2 + 1) * 512],
                                   start=(k == 0), stop=(k == 7)))
            b.wt(SG, T, tp)
            tpe = b.op(SG, lambda e, pc=pc, p=PM[psl]:
                       e.activation(projT[:, pc * 512:(pc + 1) * 512],
                                    p[:], AF.Identity,
                                    bias=bpB[:, pc:pc + 1], scale=1.0))
            sPMb[psl].release(SG, tpe)
        sPA[0].acquire(T)
        tps = 0
        for pc in range(2):
            b.sw(SG)
            if pc == 1:
                b.wt(SG, T, tps)
            tq = b.op(SG, lambda e, pc=pc:
                      e.activation(projsq[:],
                                   projT[:, pc * 512:(pc + 1) * 512],
                                   AF.Square))
            b.wt(T, SG, tq)
            tps = b.op(T, lambda e, pc=pc:
                       e.matmul(PA[0][0:1, :], onesb[:], projsq[:],
                                start=(pc == 0), stop=(pc == 1)))
        b.wt(V, T, tps)
        sl1 = sq1[:, q2 * 512:(q2 + 1) * 512]
        tmx_ = b.op(V, lambda e, a=sl1: e.tensor_single_scalar(
            a, PA[0][0:1, :], 1e-24, ALU.max))
        sPA[0].release(V, tmx_)
        b.wt(SG, V, tmx_)
        b.op(SG, lambda e, a=sl1: e.activation(a, a, AF.Ln))
        b.sw(SG)
        tex_ = b.op(SG, lambda e, a=sl1:
                    e.activation(a, a, AF.Exp, scale=-0.5))
        b.wt(V, SG, tex_)
        tinv = b.op(V, lambda e, a=sl1: e.tensor_single_scalar(
            invs[:], a, scaleT[0:1, 0:1], ALU.mult))
        b.wt(T, V, tinv)
        sPA[1].acquire(T)
        for ti in range(4):
            tti = b.op(T, lambda e, ti=ti:
                       e.transpose(PA[1][0:128, ti:ti + 1],
                                   invs[0:1, ti * 128:(ti + 1) * 128],
                                   identF[0:1, 0:1]))
        b.wt(V, T, tti)
        sgate.acquire(V)
        tiv = b.op(V, lambda e: e.tensor_copy(invT[:], PA[1][:, 0:4]))
        sPA[1].release(V, tiv)
        for ti in range(4):
            sPA[1].acquire(T)
            b.wt(T, V, tiv)
            for pc in range(2):
                tl = b.op(T, lambda e, pc=pc, ti=ti, q2=q2:
                          e.matmul(PA[1][0:128, 0:4],
                                   projT[:, pc * 512 + ti * 128:
                                         pc * 512 + (ti + 1) * 128],
                                   simnB[:, pc * 4:(pc + 1) * 4],
                                   start=(pc == 0), stop=(pc == 1)))
            if ti == 3:
                sproj.release(T, tl)
            b.wt(V, T, tl)
            tlg = b.op(V, lambda e, ti=ti:
                       e.tensor_single_scalar(lgsb[:], PA[1][:, 0:4],
                                              invT[:, ti:ti + 1],
                                              ALU.mult))
            sPA[1].release(V, tlg)
            b.sw(V)
            b.op(V, lambda e: e.reduce_max(sc(9), lgsb[:], axis=AX))
            tmx = b.op(V, lambda e: e.reduce_max(sc(10), lgsb[:], axis=AX,
                                                 negate=True))
            b.wt(SG, V, tmx)
            te = b.op(SG, lambda e: e.activation(esb[:], lgsb[:], AF.Exp,
                                                 bias=sc(10), scale=1.0))
            b.wt(V, SG, te)
            b.op(V, lambda e: e.reduce_sum(sc(11), esb[:], axis=AX))
            b.sw(V)
            b.op(V, lambda e: e.reciprocal(sc(12), sc(11)))
            b.op(V, lambda e: e.tensor_single_scalar(mskb[:], lgsb[:],
                                                     sc(9), ALU.is_equal))
            b.sw(V)
            b.op(V, lambda e: e.tensor_tensor(esb[:], mskb[:], esb[:],
                                              ALU.mult))
            b.sw(V)
            trt = b.op(V, lambda e:
                       e.tensor_single_scalar(retb[:], esb[:], sc(12),
                                              ALU.mult))
            b.wt(T, V, trt)
            sPA[1].acquire(T)
            ttr = b.op(T, lambda e: e.transpose(PA1B[0:4, 0:128],
                                                retb[:], identB[:]))
            b.wt(V, T, ttr)
            if q2 == 0 and ti == 0:
                sretT.acquire(V)
            trc = b.op(V, lambda e, q2=q2, ti=ti: e.tensor_copy(
                retT[0:4, q2 * 512 + ti * 128:
                     q2 * 512 + (ti + 1) * 128],
                PA1B[0:4, 0:128]))
            sPA[1].release(V, trc)
        sgate.release(V, trc)
        ret_tok[(sh, q2)] = trc

    def dn_block(sh, q2):
        sPA[0].acquire(T)
        b.wt(T, V, h2_tok[(sh, q2)])
        for k in range(8):
            tdn = b.op(T, lambda e, k=k, q2=q2:
                       e.matmul(PA[0][0:15, :],
                                wdT[:, k * 15:(k + 1) * 15],
                                h2T[:, k * 1024 + q2 * 512:
                                    k * 1024 + (q2 + 1) * 512],
                                start=(k == 0), stop=(k == 7)))
        b.wt(V, T, tdn)
        if q2 == 0:
            sffn.acquire(V)
        tdv = b.op(V, lambda e, q2=q2:
                   e.tensor_copy(down15[0:15, q2 * 512:(q2 + 1) * 512],
                                 PA[0][0:15, :]))
        sPA[0].release(V, tdv)
        sPA[0].acquire(T)
        b.wt(T, V, ret_tok[(sh, q2)])
        trp = b.op(T, lambda e, q2=q2:
                   e.matmul(PA[0][0:15, :], e415b[:],
                            retT[0:4, q2 * 512:(q2 + 1) * 512],
                            start=True, stop=True))
        if q2 == 1:
            sretT.release(T, trp)
        b.wt(V, T, trp)
        b.sw(V)
        tsc15 = b.op(V, lambda e, q2=q2: e.tensor_tensor(
            scaled15[0:15, q2 * 512:(q2 + 1) * 512],
            down15[0:15, q2 * 512:(q2 + 1) * 512],
            PA[0][0:15, :], ALU.mult))
        sPA[0].release(V, tsc15)
        s15_tok[(sh, q2)] = tsc15

    def wff_gen(sh, q2):
        fq = ffnsb[:, q2 * 4096:(q2 + 1) * 4096]
        b.wt(T, V, s15_tok[(sh, q2)])
        for oc in range(8):
            psl = oc % 2
            sPMb[psl].acquire(T)
            for k in range(8):
                b.op(T, lambda e, k=k, oc=oc, q2=q2, p=PM[psl]:
                     e.matmul(p[:],
                              wffT[:, k * 1024 + oc * 128:
                                   k * 1024 + (oc + 1) * 128],
                              h2T[:, k * 1024 + q2 * 512:
                                  k * 1024 + (q2 + 1) * 512],
                              start=(k == 0), stop=False))
            tf = b.op(T, lambda e, oc=oc, q2=q2, p=PM[psl]:
                      e.matmul(p[:], wuT[0:15, oc * 128:(oc + 1) * 128],
                               scaled15[0:15, q2 * 512:(q2 + 1) * 512],
                               start=False, stop=True))
            if oc == 7 and q2 == 1:
                sffn.release(T, tf)
            b.wt(SG, T, tf)
            if oc == 0:
                sffnb[q2].acquire(SG)
            tfe = b.op(SG, lambda e, oc=oc, fq=fq, p=PM[psl]:
                       e.copy(fq[:, oc * 512:(oc + 1) * 512], p[:]))
            sPMb[psl].release(SG, tfe)
            ffn_tok[(sh, q2)] = tfe
            yield

    def out_gen(sh, q2):
        fq = ffnsb[:, q2 * 4096:(q2 + 1) * 4096]
        for ti in range(4):
            t8 = q2 * 4 + ti
            psl = ti % 2
            osl = t8 % 2
            b.wt(T, SG, ffn_tok[(sh, q2)])
            sPS[psl].acquire(T)
            for oc in range(8):
                tto = b.op(T, lambda e, oc=oc, ti=ti, fq=fq, p=PSB[psl]:
                           e.transpose(p[:, oc * 128:(oc + 1) * 128],
                                       fq[:, oc * 512 + ti * 128:
                                          oc * 512 + (ti + 1) * 128],
                                       identB[:]))
            if ti == 3:
                sffnb[q2].release(T, tto)
            b.wt(V, T, tto)
            souts[osl].acquire(V)
            tout = b.op(V, lambda e, t8=t8, osl=osl, p=PSB[psl]:
                        e.tensor_tensor(
                            outsb[:, osl * 1024:(osl + 1) * 1024],
                            yb[:, t8 * 1024:(t8 + 1) * 1024],
                            p[:, 0:1024], ALU.add))
            sPS[psl].release(V, tout)
            sy[t8].release(V, tout)
            b.wt(G, V, tout)
            tod = b.dma_slot(G, "out%d" % osl,
                             [(o_d[(sh * 8 + t8) * 128:
                                   (sh * 8 + t8 + 1) * 128, :],
                               outsb[:, osl * 1024:(osl + 1) * 1024])])
            souts[osl].release(G, tod)
            yield

    def run_gen(g):
        for _ in g:
            pass

    def interleave(a, bgen):
        while True:
            done = 0
            for g in (a, bgen):
                try:
                    next(g)
                except StopIteration:
                    done += 1
            if done == 2:
                break

    prev_out = None
    for sh in range(2):
        if prev_out is None:
            run_gen(wo_gen(sh, 0))
        else:
            interleave(prev_out, wo_gen(sh, 0))
        interleave(tiles_gen(sh, 0), wo_gen(sh, 1))
        router_block(sh, 0)
        dn_block(sh, 0)
        interleave(tiles_gen(sh, 1), wff_gen(sh, 0))
        router_block(sh, 1)
        dn_block(sh, 1)
        interleave(out_gen(sh, 0), wff_gen(sh, 1))
        prev_out = out_gen(sh, 1)
    run_gen(prev_out)

    emit_taps("b")

    b.wt(G, G, b.cnt[G])
    for ek, v in list(b.scnt.items()):
        if v > 0:
            b.wtd(G, ("d", ek, v))

    with nc.Block() as block:
        b.finalize(block)
    es_b.close()
    es.close()
    return nc


_NC_CACHE = {}


def _make_consts():
    inv = np.power(np.float64(10000.0),
                   -np.arange(0, 32, 2, dtype=np.float64) / 32.0)
    t = np.arange(S, dtype=np.float64)
    ang = t[None, :] * inv[:, None]
    cos2 = np.repeat(np.cos(ang), 2, axis=0)
    sin_ = np.sin(ang)
    sin2 = np.empty((32, S), np.float64)
    sin2[0::2] = -sin_
    sin2[1::2] = sin_
    cosq = np.tile(cos2, (4, 1)).astype(np.float32)
    sinq = np.tile(sin2, (4, 1)).astype(np.float32)
    cosk = np.empty((128, 1024), np.float32)
    sink = np.empty((128, 1024), np.float32)
    for ch in range(4):
        cosk[32 * ch:32 * ch + 32] = cos2[:, ch * 1024:(ch + 1) * 1024]
        sink[32 * ch:32 * ch + 32] = sin2[:, ch * 1024:(ch + 1) * 1024]
    ident = np.eye(128, dtype=np.float32)
    p128 = np.zeros((128, 128), np.float32)
    for i in range(0, 128, 2):
        p128[i + 1, i] = 1.0
        p128[i, i + 1] = 1.0
    triu = np.triu(np.ones((128, 128), np.float32))
    e415 = np.zeros((4, 15), np.float32)
    rows = [0, 1, 1, 2, 2, 2, 2, 3, 3, 3, 3, 3, 3, 3, 3]
    for jj, ee in enumerate(rows):
        e415[ee, jj] = 1.0
    return dict(cosq=cosq, sinq=sinq, cosk=cosk, sink=sink, ident=ident,
                p128=p128, triu=triu, e415=e415)


def kernel(**inputs):
    if "nc" not in _NC_CACHE:
        _NC_CACHE["nc"] = build_program()
    nc = _NC_CACHE["nc"]
    C = _make_consts()
    asc = np.ascontiguousarray

    def g(k):
        return np.asarray(inputs[k], dtype=np.float32)

    x = g("L0_attn_x")
    wd = np.concatenate([g("L0_ffn_Wd0"), g("L0_ffn_Wd1"),
                         g("L0_ffn_Wd2"), g("L0_ffn_Wd3")], axis=0)
    wu = np.concatenate([g("L0_ffn_Wu0"), g("L0_ffn_Wu1"),
                         g("L0_ffn_Wu2"), g("L0_ffn_Wu3")], axis=1)
    in_maps = []
    for c in range(8):
        bb, hh = c // 2, c % 2
        hoff = np.zeros((1, 16), np.int32)
        for k16 in range(16):
            g8 = k16 % 8
            from_a = k16 < 8
            mine = (hh == 0) == from_a
            hoff[0, k16] = g8 * 2048 if mine else (8 + g8) * 2048
        in_maps.append(dict(
            x=asc(x[bb]),
            xres=asc(x[bb, hh * 2048:(hh + 1) * 2048]),
            wdq=g("L0_attn_W_DQ"), wdkv=g("L0_attn_W_DKV"),
            wkr=g("L0_attn_W_KR"),
            wuq=asc(g("L0_attn_W_UQ")[:, hh * 512:(hh + 1) * 512]),
            wqr=asc(g("L0_attn_W_QR")[:, hh * 256:(hh + 1) * 256]),
            wuk=asc(g("L0_attn_W_UK")[:, hh * 512:(hh + 1) * 512]),
            wuv=asc(g("L0_attn_W_UV")[:, hh * 512:(hh + 1) * 512]),
            wo=g("L0_attn_W_O"), wff=g("L0_ffn_W"), wp=g("L0_ffn_Wp"),
            bp=g("L0_ffn_bp"), sim=g("L0_ffn_sim"), temp=g("L0_ffn_temp"),
            wd=wd, wu=wu, hoff=hoff, **C,
        ))
    res = run_bass_kernel_spmd(nc, in_maps, core_ids=list(range(8)))
    _NC_CACHE["res"] = res
    out = np.empty((4, S, D), np.float32)
    for c in range(8):
        bb, hh = c // 2, c % 2
        out[bb, hh * 2048:(hh + 1) * 2048] = res.results[c]["o"]
    return out
